# version 13
# baseline (speedup 1.0000x reference)
import numpy as np
import ml_dtypes
import concourse.bass as bass
import concourse.mybir as mybir
from concourse.bass_utils import run_bass_kernel_spmd

F32 = mybir.dt.float32
BF16 = mybir.dt.bfloat16
I32 = mybir.dt.int32
AF = mybir.ActivationFunctionType
ALU = mybir.AluOpType
AX = mybir.AxisListType
BF = ml_dtypes.bfloat16

NCORE = 8
D = 1024
KC = 8
CH_T = 1024
NT = 8
NCH = 4
H = 16
HKV = 4
HD = 64
DKV = 256
DF1 = 4096
M1 = 32
DF2 = 3072
M2 = 24
EPS = 1e-5
QSC = 1.0 / 127.0


TAGMAP = {}


class Prog:
    def __init__(self):
        self.items = []

    def op(self, e, fn, deps=(), tag=""):
        return self._add(e, fn, deps, None, tag)

    def dma(self, q, sem, fn, deps=(), tag=""):
        assert q in ("Y", "S", "P")
        return self._add(q, fn, deps, sem, tag)

    def _add(self, e, fn, deps, dsem, tag):
        i = len(self.items)
        dd = set()
        for d in deps:
            if d is None:
                continue
            if isinstance(d, (list, tuple, set)):
                dd.update(x for x in d if x is not None)
            else:
                dd.add(d)
        self.items.append(dict(i=i, e=e, fn=fn, deps=sorted(dd), dsem=dsem,
                               tag=tag))
        return i

    def build(self, nc):
        targets = set()
        for it in self.items:
            targets.update(it["deps"])
        cnt = {}
        for it in self.items:
            s = it["dsem"] if it["dsem"] else ("E" + it["e"])
            it["sem"] = s
            if it["dsem"] is not None or (it["i"] in targets):
                amt = 16 if it["dsem"] else 1
                cnt[s] = cnt.get(s, 0) + amt
                it["inc"] = True
                it["val"] = cnt[s]
            else:
                it["inc"] = False
                it["val"] = None
        semh = {s: nc.alloc_semaphore(f"sm_{s}") for s in cnt}
        by_e = {}
        for it in self.items:
            by_e.setdefault(it["e"], []).append(it)
        idx = self.items

        def emit(eng, elist):
            water = {}
            for it in elist:
                for d in it["deps"]:
                    dit = idx[d]
                    assert dit["inc"], f"dep target {d} ({dit['tag']}) lacks inc"
                    s, v = dit["sem"], dit["val"]
                    if water.get(s, 0) < v:
                        eng.wait_ge(semh[s], v)
                        water[s] = v
                ins = it["fn"](eng)
                try:
                    TAGMAP[ins.ins.name] = (it["e"], it["tag"])
                except Exception:
                    try:
                        TAGMAP[ins.name] = (it["e"], it["tag"])
                    except Exception:
                        pass
                if it["inc"]:
                    ins.then_inc(semh[it["sem"]], 16 if it["dsem"] else 1)

        EMAP = dict(T="tensor", V="vector", S="scalar", P="gpsimd", Y="sync")
        with nc.Block() as blk:
            for e in ("T", "V", "S", "P", "Y"):
                if e in by_e:
                    getattr(blk, EMAP[e])(lambda eng, el=by_e[e]: emit(eng, el))
        return semh


class Res:
    __slots__ = ("w", "rs")

    def __init__(self):
        self.w = None
        self.rs = []

    def rdep(self):
        return [self.w] if self.w is not None else []

    def wdep(self):
        d = list(self.rs)
        if self.w is not None:
            d.append(self.w)
        return d

    def read(self, tok):
        self.rs.append(tok)
        return tok

    def write(self, tok):
        self.w = tok
        self.rs = []
        return tok


def host_consts():
    c = {}
    c["ident_bf"] = np.asarray(np.eye(128, dtype=np.float32), dtype=BF)
    c["ident_f32"] = np.eye(128, dtype=np.float32)
    c["ones1"] = np.asarray(np.ones((1, 128), dtype=np.float32), dtype=BF)
    half = HD // 2
    theta = 1.0 / 10000.0 ** (np.arange(half, dtype=np.float64) / half)
    ang = np.outer(np.arange(2048, dtype=np.float64), theta)
    cosv, sinv = np.cos(ang), np.sin(ang)
    ct = np.tile(np.concatenate([cosv, cosv], axis=1), (1, H))
    st = np.tile(np.concatenate([-sinv, sinv], axis=1), (1, H))
    c["costab"] = np.asarray(ct, dtype=np.float32)
    c["sintab"] = np.asarray(st, dtype=np.float32)
    r = np.arange(128)[:, None]
    cc = np.arange(384)[None, :]
    c["band3"] = np.asarray(((cc > r) & (cc <= r + 256)).astype(np.float32),
                            dtype=BF)
    c["band2"] = np.asarray((cc[:, :256] <= r + 128).astype(np.float32), dtype=BF)
    c["band1"] = np.asarray((cc[:, :128] <= r).astype(np.float32), dtype=BF)
    return c


DEV = {"nch": NCH, "stop": None, "taps": ()}


def build_program():
    nc = bass.Bass("TRN2", target_bir_lowering=False)
    P = Prog()
    dev_nch = DEV.get("nch", NCH)
    dev_stop = DEV.get("stop")
    tap_specs = []

    class StopBuild(Exception):
        pass

    def phase_end(name, ch, toks):
        if dev_stop == name and (name == "conv" or ch == dev_nch - 1):
            tap_specs.append(list(toks) if isinstance(toks, (list, tuple))
                             else [toks])
            raise StopBuild()

    xs = nc.dram_tensor("xs", [NCH * CH_T, D], F32, kind="ExternalInput").ap()
    y = nc.dram_tensor("y", [NCH * CH_T, D], F32, kind="ExternalOutput").ap()
    W = {}
    for nm, shp, dt in [
        ("wq", [D, D], F32), ("wk", [D, DKV], F32), ("wv", [D, DKV], F32),
        ("wo", [D, D], F32), ("wf1", [D, DF1], F32), ("wf2", [DF1, D], F32),
        ("wg", [DF2, D], I32), ("wu", [DF2, D], I32), ("wd", [D, DF2], I32),
        ("bq", [1, D], F32), ("bk", [1, DKV], F32), ("bv", [1, DKV], F32),
        ("bo", [1, D], F32), ("bf1", [1, DF1], F32), ("bf2", [1, D], F32),
        ("g1", [1, D], F32), ("b1", [1, D], F32), ("g2", [1, D], F32),
        ("b2", [1, D], F32),
    ]:
        W[nm] = nc.dram_tensor(nm, shp, dt, kind="ExternalInput").ap()
    C = {}
    for nm, shp in [("ident_bf", [128, 128]), ("ones1", [1, 128]),
                    ("band3", [128, 384]), ("band2", [128, 256]),
                    ("band1", [128, 128])]:
        C[nm] = nc.dram_tensor(nm, shp, BF16, kind="ExternalInput").ap()
    for nm in ("costab", "sintab"):
        C[nm] = nc.dram_tensor(nm, [2048, D], F32, kind="ExternalInput").ap()
    C["ident_f32"] = nc.dram_tensor("ident_f32", [128, 128], F32,
                                    kind="ExternalInput").ap()

    s_wq = nc.dram_tensor("s_wq", [D, D], BF16).ap()
    s_wk = nc.dram_tensor("s_wk", [D, DKV], BF16).ap()
    s_wv = nc.dram_tensor("s_wv", [D, DKV], BF16).ap()
    s_wo = nc.dram_tensor("s_wo", [D, D], BF16).ap()
    s_wf1p = nc.dram_tensor("s_wf1p", [M1, 128, D], BF16).ap()
    s_wf2 = nc.dram_tensor("s_wf2", [DF1, D], BF16).ap()
    s_wgt = nc.dram_tensor("s_wgt", [M2, 128, D], BF16).ap()
    s_wut = nc.dram_tensor("s_wut", [M2, 128, D], BF16).ap()
    s_wdt = nc.dram_tensor("s_wdt", [DF2, D], BF16).ap()

    BASE = 18560
    off = BASE

    def At(name, shape, dtype, o=None):
        nonlocal off
        esz = 2 if dtype == BF16 else 4
        nbytes = int(np.prod(shape[1:])) * esz
        if o is None:
            o = off
            off = (off + nbytes + 31) & ~31
        return nc.alloc_sbuf_tensor_at(name, list(shape), dtype, offset=o)

    B0 = At("B0", [128, NT * 1024], F32)
    B1 = At("B1", [128, NT * 1024], F32)
    HT = At("HT", [128, KC * 1024], BF16)
    STATS = At("STATS", [128, 64], F32)
    SMX = At("SMX", [128, 96], F32)
    BAND3 = At("BAND3", [128, 384], BF16)
    BAND2 = At("BAND2", [128, 256], BF16)
    BAND1 = At("BAND1", [128, 128], BF16)
    IDB = At("IDB", [128, 128], BF16)
    IDF = At("IDF", [128, 128], F32)
    ONE1 = At("ONE1", [1, 128], BF16)
    ROWB = At("ROWB", [1, 4096], BF16)
    ROWL = At("ROWL", [1, 4096], BF16)
    G1C = At("G1C", [128, 8], F32)
    B1C = At("B1C", [128, 8], F32)
    G2C = At("G2C", [128, 8], F32)
    B2C = At("B2C", [128, 8], F32)
    B1CB = At("B1CB", [128, 8], BF16)
    B2CB = At("B2CB", [128, 8], BF16)
    BF1C = At("BF1C", [128, 32], F32)
    KTH = At("KTH", [128, 512], BF16)
    VTH = At("VTH", [128, 512], BF16)
    ARENA = (off + 127) & ~127
    ARENA_SZ = 229344 - ARENA
    assert ARENA_SZ >= 102400, f"arena too small: {ARENA_SZ}"

    def Ar(name, shape, dtype, o):
        esz = 2 if dtype == BF16 else 4
        assert o + int(np.prod(shape[1:])) * esz <= ARENA_SZ, name
        return At(name, shape, dtype, o=ARENA + o)

    QT = Ar("QT", [128, KC * 1024], BF16, 0)
    CTB = Ar("CTB", [128, 2 * 1024], F32, 16384)
    STB = Ar("STB", [128, 2 * 1024], F32, 24576)
    WQSB = Ar("WQSB", [128, KC * 1024], BF16, 32768)
    WOSB = Ar("WOSB", [128, KC * 1024], BF16, 16384)
    WKSB = Ar("WKSB", [128, KC * 256], BF16, 49152)
    WVSB = Ar("WVSB", [128, KC * 256], BF16, 53248)
    RS = Ar("RS", [128, 2 * 1024], F32, 57344)
    KT = Ar("KT", [128, 2 * 10 * 128], BF16, 65536)
    VT = Ar("VT", [128, 10 * 256], BF16, 70656)
    PMASK = Ar("PMASK", [128, 16 * 384], BF16, 75776)
    PT = Ar("PT", [128, 4 * 384], BF16, 88064)
    OTB = Ar("OTB", [128, 2 * 1024], BF16, 91136)
    DIAG = Ar("DIAG", [128, 2 * 128], BF16, 95232)
    KTM = Ar("KTM", [128, 2 * 256], BF16, 95744)
    TMS1 = Ar("TMS1", [128, 2 * 1024], BF16, 96768)
    DMP1 = Ar("DMP1", [128, 1024], F32, 100864)
    WF2SB = Ar("WF2SB", [128, 16 * 1024], BF16, 32768)
    GT = Ar("GT", [128, 16 * 1024], BF16, 65536)
    PAN1 = Ar("PAN1", [128, 2 * 1024], BF16, 100864)
    FFACC = Ar("FFACC", [128, NT * 1024], F32, 0)
    TMS2 = Ar("TMS2", [128, 2 * 1024], BF16, 0)
    DMP2 = DMP1
    HT2 = Ar("HT2", [128, M2 * 1024], BF16, 0)
    WDSB = Ar("WDSB", [128, M2 * 1024], BF16, 49152)
    SIL = Ar("SIL", [128, 2 * 512], BF16, 98304)
    PAN2 = At("PAN2", [128, 4 * 1024], BF16, o=BASE)
    OS = Ar("OS", [128, 2 * 1024], F32, 8192)
    DMP4 = DMP1
    STG = Ar("STG", [128, KC * 4096], BF16, 0)
    CSA = Ar("CSA", [128, 4096], BF16, 65536)
    CSB = Ar("CSB", [128, 4096], BF16, 73728)
    CS2 = Ar("CS2", [128, 1024], BF16, 81920)
    CS2B = Ar("CS2B", [128, 1024], BF16, 83968)
    ROWF = Ar("ROWF", [1, 4096], F32, 86016)
    ROWF2 = Ar("ROWF2", [1, 1024], F32, 102400)
    AW = Ar("AW", [128, M2 * 1024], BF16, 0)
    CSC = Ar("CSC", [128, 4096], BF16, 49152)
    CSD = Ar("CSD", [128, 4096], BF16, 57344)

    pbk = [nc.alloc_psum_tensor(f"pbk{i}", [128, 512], F32) for i in range(5)]
    ptr2 = [nc.alloc_psum_tensor(f"ptr{i}", [128, 1024], BF16) for i in range(2)]
    paux = nc.alloc_psum_tensor("paux", [128, 512], F32)

    Rpb = [Res() for _ in range(5)]
    Rptr2 = [Res(), Res()]
    Raux = Res()
    RB0, RB1 = Res(), Res()
    RHT, RQT, RKT, RVT, RRS = Res(), Res(), Res(), Res(), Res()
    RPM = [Res() for _ in range(16)]
    RPT = [Res(), Res(), Res(), Res()]
    ROT = [Res(), Res()]
    RDIAG = [Res(), Res()]
    RKTM = [Res(), Res()]
    RTMS = [Res(), Res()]
    RSTATS = Res()
    RGT, RWF2, RFF = Res(), Res(), Res()
    RPAN1 = [Res(), Res()]
    RHT2, RWD = Res(), Res()
    RPAN2 = [Res(), Res(), Res(), Res()]
    RSIL = [Res(), Res()]
    ROS = [Res(), Res()]
    RCS, RCS2, RSTG, RAW = Res(), Res(), Res(), Res()
    RCSB, RCS2B = Res(), Res()
    RCSC, RCSD = Res(), Res()
    RROWF, RROWF2 = Res(), Res()
    RWQ, RWO, RWK, RWV, RCT, RST = Res(), Res(), Res(), Res(), Res(), Res()
    RKTH, RVTH = Res(), Res()
    RDMP = Res()

    tcst = []
    for dst, src in [(IDB, C["ident_bf"]), (IDF, C["ident_f32"]),
                     (ONE1, C["ones1"]), (BAND3, C["band3"]),
                     (BAND2, C["band2"]), (BAND1, C["band1"])]:
        tcst.append(P.dma("P", "scv", lambda e, d=dst, s=src: e.dma_start(d[:], s),
                          tag="cst"))

    def colload(dst, src, n, slot):
        tr_ = P.dma("P", "scv", lambda e, s=src: e.dma_start(
            ROWF[0:1, 0:1024], s), deps=RROWF.wdep(), tag="colrow")
        RROWF.write(tr_)
        trl = None
        for j in range(n):
            dep = [tr_, tcst[1]]
            if j == 0:
                dep += Rpb[2].wdep()
            trl = P.op("T", lambda e, jj=j: e.transpose(
                pbk[2][:, jj:jj + 1], ROWF[0:1, jj * 128:(jj + 1) * 128],
                IDF[0:1, 0:1]), deps=dep, tag="coltr")
        Rpb[2].write(trl)
        tc = P.op("V", lambda e, d=dst: e.tensor_copy(d[:, 0:n],
                                                      pbk[2][:, 0:n]),
                  deps=[trl], tag="colev")
        Rpb[2].read(tc)
        RROWF.read(tc)
        return tc

    tg1 = colload(G1C, W["g1"], 8, 0)
    tb1 = colload(B1C, W["b1"], 8, 1)
    tg2 = colload(G2C, W["g2"], 8, 2)
    tb2 = colload(B2C, W["b2"], 8, 3)

    COLS4 = [tg1, tb1, tg2, tb2]
    tb1b = P.op("V", lambda e: e.tensor_copy(B1CB[:], B1C[:]), deps=COLS4,
                tag="b1cb")
    tb2b = P.op("V", lambda e: e.tensor_copy(B2CB[:], B2C[:]), deps=COLS4,
                tag="b2cb")

    def conv_attn_w(wsrc, wdst, n_out, gtok, gcol, extra_scale, bsrc, brow_off,
                    bscale, row_off_f):
        nstr = max(1, n_out // 512)
        strw = min(512, n_out)
        tbr = P.dma("P", "srow", lambda e: e.dma_start(
            ROWF[0:1, row_off_f:row_off_f + n_out], bsrc), deps=RROWF.wdep(),
            tag="brow")
        RROWF.write(tbr)
        out_toks = []
        mm_last = [None] * nstr
        for kc in range(KC):
            buf = CSA if kc % 2 == 0 else CSB
            rbuf = RCS if kc % 2 == 0 else RCSB
            sem = "scv2a" if kc % 2 == 0 else "scv2b"
            tl = P.dma("P", sem, lambda e, k=kc, b=buf: e.dma_start(
                b[:, 0:n_out], wsrc[k * 128:(k + 1) * 128, :]),
                deps=rbuf.wdep(), tag="wload")
            rbuf.write(tl)
            fold_toks = []
            for s in range(nstr):
                dep = [tl, tb1b]
                if kc == 0:
                    dep += Rpb[3 + s].wdep()
                else:
                    dep.append(mm_last[s])
                tm = P.op("T", lambda e, k=kc, ss=s, b=buf: e.matmul(
                    pbk[3 + ss][0:1, 0:strw], B1CB[:, k:k + 1],
                    b[:, ss * strw:ss * strw + strw],
                    start=(k == 0), stop=(k == KC - 1)), deps=dep, tag="bfold")
                rbuf.read(tm)
                mm_last[s] = tm
                fold_toks.append(tm)
            if gcol is not None:
                ts_ = P.op("V", lambda e, k=kc, b=buf: e.tensor_scalar(
                    out=b[:, 0:n_out], in0=b[:, 0:n_out],
                    scalar1=gcol[:, k:k + 1], scalar2=float(extra_scale),
                    op0=ALU.mult, op1=ALU.mult),
                    deps=[tl] + gtok + fold_toks, tag="wscale")
                rbuf.write(ts_)
                tl = ts_
            tw = P.dma("Y", "scw", lambda e, k=kc, b=buf: e.dma_start(
                wdst[k * 128:(k + 1) * 128, :], b[:, 0:n_out]), deps=[tl],
                tag="wstore")
            rbuf.read(tw)
            out_toks.append(tw)
        for s in range(nstr):
            Rpb[3 + s].write(mm_last[s])
            te = P.op("V", lambda e, ss=s: e.scalar_tensor_tensor(
                out=ROWF2[0:1, 0:strw],
                in0=ROWF[0:1, row_off_f + ss * strw:row_off_f + ss * strw + strw],
                scalar=1.0, in1=pbk[3 + ss][0:1, 0:strw],
                op0=ALU.mult, op1=ALU.add),
                deps=[mm_last[s], tbr] + RROWF2.wdep(), tag="bevac")
            tsc = P.op("V", lambda e, ss=s: e.tensor_scalar(
                out=ROWF2[0:1, 0:strw], in0=ROWF2[0:1, 0:strw],
                scalar1=float(bscale), scalar2=None, op0=ALU.mult),
                deps=[te], tag="bscale")
            te = tsc
            Rpb[3 + s].read(te)
            RROWF2.write(te)
            tc = P.op("V", lambda e, ss=s: e.tensor_copy(
                ROWB[0:1, brow_off + ss * strw:brow_off + ss * strw + strw],
                ROWF2[0:1, 0:strw]), deps=[te], tag="bcast")
            tlo = P.op("V", lambda e, ss=s: e.tensor_tensor(
                out=ROWL[0:1, brow_off + ss * strw:brow_off + ss * strw + strw],
                in0=ROWF2[0:1, 0:strw],
                in1=ROWB[0:1, brow_off + ss * strw:brow_off + ss * strw + strw],
                op=ALU.subtract), deps=[tc], tag="blo")
            RROWF2.read(tlo)
            out_toks.append(tlo)
        return out_toks

    cv_wq = conv_attn_w(W["wq"], s_wq, D, COLS4, G1C, 0.125, W["bq"], 0, 0.125, 0)
    cv_wk = conv_attn_w(W["wk"], s_wk, DKV, COLS4, G1C, 1.0, W["bk"], 3072, 1.0, 1024)
    cv_wv = conv_attn_w(W["wv"], s_wv, DKV, COLS4, G1C, 1.0, W["bv"], 3328, 1.0, 1280)

    cv_wo = []
    for kc in range(KC):
        buf = CSA if kc % 2 == 0 else CSB
        rbuf = RCS if kc % 2 == 0 else RCSB
        sem = "scv2a" if kc % 2 == 0 else "scv2b"
        tl = P.dma("P", sem, lambda e, k=kc, b=buf: e.dma_start(
            b[:, 0:D], W["wo"][k * 128:(k + 1) * 128, :]), deps=rbuf.wdep(),
            tag="woload")
        rbuf.write(tl)
        tw = P.dma("Y", "scw", lambda e, k=kc, b=buf: e.dma_start(
            s_wo[k * 128:(k + 1) * 128, :], b[:, 0:D]), deps=[tl], tag="wostore")
        rbuf.read(tw)
        cv_wo.append(tw)
    tbo = P.dma("P", "srow", lambda e: e.dma_start(ROWF[0:1, 1536:2560], W["bo"]),
                deps=RROWF.wdep(), tag="borow")
    RROWF.write(tbo)
    tboc = P.op("V", lambda e: e.tensor_copy(ROWB[0:1, 1024:2048],
                                             ROWF[0:1, 1536:2560]), deps=[tbo],
                tag="bocast")
    tboc = P.op("V", lambda e: e.tensor_tensor(
        out=ROWL[0:1, 1024:2048], in0=ROWF[0:1, 1536:2560],
        in1=ROWB[0:1, 1024:2048], op=ALU.subtract), deps=[tboc], tag="bolo")
    RROWF.read(tboc)
    tbf2 = P.dma("P", "srow", lambda e: e.dma_start(ROWF[0:1, 2560:3584],
                                                   W["bf2"]),
                 deps=RROWF.wdep(), tag="bf2row")
    RROWF.write(tbf2)
    tbf2c = P.op("V", lambda e: e.tensor_copy(ROWB[0:1, 2048:3072],
                                              ROWF[0:1, 2560:3584]), deps=[tbf2],
                 tag="bf2cast")
    tbf2c = P.op("V", lambda e: e.tensor_tensor(
        out=ROWL[0:1, 2048:3072], in0=ROWF[0:1, 2560:3584],
        in1=ROWB[0:1, 2048:3072], op=ALU.subtract), deps=[tbf2c], tag="bf2lo")
    RROWF.read(tbf2c)
    ROWS_DONE = cv_wq + cv_wk + cv_wv + [tboc, tbf2c] + tcst

    stg_loads = []
    for kc in range(KC):
        stg_loads.append(P.dma("P", f"sstg{kc}", lambda e, k=kc: e.dma_start(
            STG[:, k * 4096:(k + 1) * 4096], W["wf1"][k * 128:(k + 1) * 128, :]),
            deps=(RSTG.wdep() if kc == 0 else []), tag="wf1load"))
    tbf1 = P.dma("P", "srow", lambda e: e.dma_start(ROWF[0:1, 0:4096], W["bf1"]),
                 deps=RROWF.wdep() + [tboc, tbf2c], tag="bf1row")
    RROWF.write(tbf1)
    bf1_toks = []
    bf1_mms = []
    bf1_bykc = [[] for _ in range(KC)]
    for s in range(8):
        mm_last = None
        for kc in range(KC):
            dep = [stg_loads[kc], tb2b]
            if kc == 0:
                dep += Rpb[3].wdep()
            mm_last = P.op("T", lambda e, k=kc, ss=s: e.matmul(
                pbk[3][0:1, 0:512], B2CB[:, k:k + 1],
                STG[:, k * 4096 + ss * 512:k * 4096 + ss * 512 + 512],
                start=(k == 0), stop=(k == KC - 1)), deps=dep, tag="bf1fold")
            bf1_mms.append(mm_last)
            bf1_bykc[kc].append(mm_last)
        Rpb[3].write(mm_last)
        te = P.op("V", lambda e, ss=s: e.scalar_tensor_tensor(
            out=ROWF2[0:1, 0:512], in0=ROWF[0:1, ss * 512:ss * 512 + 512],
            scalar=1.0, in1=pbk[3][0:1, 0:512], op0=ALU.mult, op1=ALU.add),
            deps=[mm_last, tbf1] + RROWF2.wdep(), tag="bf1evac")
        Rpb[3].read(te)
        RROWF2.write(te)
        trl = None
        for j in range(4):
            dep = [te] + tcst
            if j == 0:
                dep += Rpb[4].wdep()
            trl = P.op("T", lambda e, ss=s, jj=j: e.transpose(
                pbk[4][:, jj:jj + 1],
                ROWF2[0:1, jj * 128:(jj + 1) * 128], IDF[0:1, 0:1]), deps=dep,
                tag="bf1tr")
        RROWF2.read(trl)
        Rpb[4].write(trl)
        tc4 = P.op("V", lambda e, ss=s: e.tensor_copy(
            BF1C[:, ss * 4:(ss + 1) * 4], pbk[4][:, 0:4]), deps=[trl],
            tag="bf1col")
        Rpb[4].read(tc4)
        bf1_toks.append(tc4)
    tbf1c = bf1_toks[-1]
    stg_toks = []
    for kc in range(KC):
        ts_ = P.op("V", lambda e, k=kc: e.tensor_scalar(
            out=STG[:, k * 4096:(k + 1) * 4096],
            in0=STG[:, k * 4096:(k + 1) * 4096],
            scalar1=G2C[:, k:k + 1], scalar2=1.0, op0=ALU.mult, op1=ALU.mult),
            deps=[stg_loads[kc]] + COLS4 + bf1_bykc[kc], tag="wf1scale")
        stg_toks.append(ts_)
    RSTG.write(stg_toks[-1])
    stg_v = STG[:].rearrange("p (kc m) -> p kc m", m=4096)
    cv_wf1 = [tbf1c]
    for m in range(M1):
        tw = P.dma("Y", "scw", lambda e, mm=m: e.dma_start(
            s_wf1p[mm].rearrange("p (kc j) -> p kc j", j=128),
            stg_v[:, :, mm * 128:(mm + 1) * 128]), deps=[RSTG.w], tag="wf1exp")
        RSTG.read(tw)
        cv_wf1.append(tw)

    cv_wf2 = []
    prev = [RCS.wdep() + cv_wf1, RCSB.wdep() + cv_wf1]
    for fc in range(M1):
        sl = fc % 2
        buf = CSA if sl == 0 else CSB
        tl = P.dma("P", "scv2a" if sl == 0 else "scv2b",
                   lambda e, f=fc, b=buf: e.dma_start(
            b[:, 0:1024], W["wf2"][f * 128:(f + 1) * 128, :]),
            deps=prev[sl], tag="wf2load")
        tw = P.dma("Y", "scw", lambda e, f=fc, b=buf: e.dma_start(
            s_wf2[f * 128:(f + 1) * 128, :], b[:, 0:1024]),
            deps=[tl], tag="wf2store")
        prev[sl] = [tw]
        cv_wf2.append(tw)
    RCS.write(cv_wf2[-1])
    RCSB.write(cv_wf2[-2])

    CBUFS = [CSA, CSB, CSC, CSD]
    CRES = [RCS, RCSB, RCSC, RCSD]
    CSEMS = ["scv2a", "scv2b", "scv2c", "scv2d"]

    def conv_t_weight(wsrc, wdst, npan):
        toks = []
        for hp in range(npan):
            par = hp % 2
            b4 = hp % 4
            buf = CBUFS[b4]
            rbuf = CRES[b4]
            obuf = CS2 if par == 0 else CS2B
            orb = RCS2 if par == 0 else RCS2B
            tl = P.dma("P", CSEMS[b4],
                       lambda e, h=hp, b=buf: e.dma_start(
                b[:, 0:D], wsrc[h * 128:(h + 1) * 128, :]),
                deps=rbuf.wdep() + (cv_wf1 if hp in (2, 3) else []),
                tag="qload")
            rbuf.write(tl)
            trl = None
            for j in range(KC):
                dep = [tl, tcst[0]]
                if j == 0:
                    dep += Rptr2[par].wdep()
                trl = P.op("T", lambda e, jj=j, b=buf, pp=par: e.transpose(
                    ptr2[pp][:, jj * 128:(jj + 1) * 128],
                    b[:, jj * 128:(jj + 1) * 128], IDB[:]), deps=dep, tag="qtr")
            Rptr2[par].write(trl)
            tev = P.op("V", lambda e, o=obuf, pp=par: e.tensor_copy(
                o[:], ptr2[pp][:]), deps=[trl] + orb.wdep(), tag="qevac")
            Rptr2[par].read(tev)
            orb.write(tev)
            rbuf.read(tev)
            tw = P.dma("Y", "scw", lambda e, h=hp, o=obuf: e.dma_start(
                wdst[h], o[:]), deps=[tev], tag="qstore")
            orb.read(tw)
            toks.append(tw)
        return toks

    cv_wg = conv_t_weight(W["wg"], s_wgt, M2)
    cv_wu = conv_t_weight(W["wu"], s_wut, M2)

    aw_toks = []
    aw_first = RAW.wdep()
    for np_ in range(KC):
        par = np_ % 2
        b4 = np_ % 4
        buf = CBUFS[b4]
        rbuf = CRES[b4]
        tl = P.dma("P", CSEMS[b4],
                   lambda e, n=np_, b=buf: e.dma_start(
            b[:, 0:DF2], W["wd"][n * 128:(n + 1) * 128, :]),
            deps=rbuf.wdep() + (cv_wf1 if np_ in (2, 3) else []),
            tag="wdload")
        rbuf.write(tl)
        for grp in range(3):
            gpar = (np_ * 3 + grp) % 2
            trl = None
            for jj in range(8):
                kf = grp * 8 + jj
                dep = [tl]
                if jj == 0:
                    dep += Rptr2[gpar].wdep()
                trl = P.op("T", lambda e, k=kf, j=jj, b=buf, pp=gpar: e.transpose(
                    ptr2[pp][:, j * 128:(j + 1) * 128],
                    b[:, k * 128:(k + 1) * 128], IDB[:]), deps=dep, tag="wdtr")
            Rptr2[gpar].write(trl)
            aw_v = AW[:].rearrange("p (kf t) -> p kf t", t=1024)
            tev = P.op("V", lambda e, g=grp, n=np_, pp=gpar: e.tensor_copy(
                aw_v[:, g * 8:(g + 1) * 8, n * 128:(n + 1) * 128],
                ptr2[pp][:].rearrange("p (a b) -> p a b", b=128)),
                deps=[trl] + (aw_first if np_ == 0 and grp == 0 else []),
                tag="wdevac")
            Rptr2[gpar].read(tev)
            rbuf.read(tev)
            aw_toks.append(tev)
    RAW.write(aw_toks[-1])
    cv_wd = []
    for kf in range(M2):
        tw = P.dma("Y", "scw", lambda e, k=kf: e.dma_start(
            s_wdt[k * 128:(k + 1) * 128, :], AW[:, k * 1024:(k + 1) * 1024]),
            deps=aw_toks, tag="wdstore")
        RAW.read(tw)
        cv_wd.append(tw)

    CONV_DONE = (cv_wq + cv_wk + cv_wv + cv_wo + cv_wf1 + cv_wf2 + cv_wg +
                 cv_wu + cv_wd + ROWS_DONE + [tbf1c] + tcst + [tg1, tb1, tg2, tb2])

    def ln_stat_tile(src, t, dmp, deps, first):
        sl = src[:, t * 1024:(t + 1) * 1024]
        sdep = RSTATS.wdep() if first else []
        t1 = P.op("V", lambda e, s=sl, tt=t: e.reduce_sum(
            STATS[:, tt:tt + 1], s, axis=AX.X),
            deps=deps + sdep, tag="lnsum")
        t2 = P.op("S", lambda e, s=sl, tt=t, d=dmp: e.activation(
            d[:], s, AF.Square, accum_out=STATS[:, 8 + tt:9 + tt]),
            deps=deps + sdep + RDMP.wdep(), tag="lnsq")
        RDMP.write(t2)
        return [t1, t2]

    def ln_stats_finish(toks, want_nmr=False, lo=0, hi=8):
        n = hi - lo
        tmu = P.op("V", lambda e: e.tensor_scalar(
            out=STATS[:, 16 + lo:16 + hi], in0=STATS[:, lo:hi],
            scalar1=1.0 / 1024.0,
            scalar2=None, op0=ALU.mult), deps=toks, tag="lnmu")
        te2 = P.op("V", lambda e: e.tensor_scalar(
            out=STATS[:, 24 + lo:24 + hi], in0=STATS[:, 8 + lo:8 + hi],
            scalar1=1.0 / 1024.0,
            scalar2=None, op0=ALU.mult), deps=toks, tag="lne2")
        tm2 = P.op("V", lambda e: e.tensor_tensor(
            out=STATS[:, 32 + lo:32 + hi], in0=STATS[:, 16 + lo:16 + hi],
            in1=STATS[:, 16 + lo:16 + hi],
            op=ALU.mult), deps=[tmu], tag="lnm2")
        tvar = P.op("V", lambda e: e.scalar_tensor_tensor(
            out=STATS[:, 24 + lo:24 + hi], in0=STATS[:, 24 + lo:24 + hi],
            scalar=float(EPS),
            in1=STATS[:, 32 + lo:32 + hi], op0=ALU.add, op1=ALU.subtract),
            deps=[te2, tm2], tag="lnvar")
        tsd = P.op("S", lambda e: e.activation(
            STATS[:, 32 + lo:32 + hi], STATS[:, 24 + lo:24 + hi], AF.Sqrt),
            deps=[tvar], tag="lnsd")
        trv0 = P.op("V", lambda e: e.reciprocal(
            STATS[:, 40 + lo:40 + hi], STATS[:, 32 + lo:32 + hi]),
            deps=[tsd], tag="lnrinv")
        tn1 = P.op("V", lambda e: e.tensor_tensor(
            out=STATS[:, 56 + lo:56 + hi], in0=STATS[:, 24 + lo:24 + hi],
            in1=STATS[:, 40 + lo:40 + hi],
            op=ALU.mult), deps=[trv0], tag="lnnw1")
        tn2 = P.op("V", lambda e: e.tensor_tensor(
            out=STATS[:, 56 + lo:56 + hi], in0=STATS[:, 56 + lo:56 + hi],
            in1=STATS[:, 40 + lo:40 + hi],
            op=ALU.mult), deps=[tn1], tag="lnnw2")
        tn3 = P.op("V", lambda e: e.tensor_scalar(
            out=STATS[:, 56 + lo:56 + hi], in0=STATS[:, 56 + lo:56 + hi],
            scalar1=-0.5,
            scalar2=1.5, op0=ALU.mult, op1=ALU.add), deps=[tn2], tag="lnnw3")
        trv = P.op("V", lambda e: e.tensor_tensor(
            out=STATS[:, 40 + lo:40 + hi], in0=STATS[:, 40 + lo:40 + hi],
            in1=STATS[:, 56 + lo:56 + hi],
            op=ALU.mult), deps=[tn3], tag="lnnw4")
        RSTATS.write(trv)
        if want_nmr:
            tnm = P.op("V", lambda e: e.scalar_tensor_tensor(
                out=STATS[:, 48 + lo:48 + hi], in0=STATS[:, 16 + lo:16 + hi],
                scalar=-1.0,
                in1=STATS[:, 40 + lo:40 + hi], op0=ALU.mult, op1=ALU.mult),
                deps=[trv],
                tag="lnnmr")
            RSTATS.read(tnm)
            return trv, tnm
        return trv, None

    def flip_tile(srcbuf, srccols, tt, dst, deps, tag):
        pb_ = tt % 2
        trl = None
        for j in range(KC):
            dep = list(deps)
            if j == 0:
                dep += Rptr2[pb_].wdep()
            trl = P.op("T", lambda e, jj=j, b=pb_: e.transpose(
                ptr2[b][:, jj * 128:(jj + 1) * 128],
                srcbuf[:, srccols + jj * 128:srccols + (jj + 1) * 128], IDB[:]),
                deps=dep, tag=tag + "tr")
        Rptr2[pb_].write(trl)
        dv = dst[:].rearrange("p (j t) -> p j t", t=1024)
        tev = P.op("V", lambda e, tt_=tt, b=pb_: e.tensor_copy(
            dv[:, :, tt_ * 128:(tt_ + 1) * 128],
            ptr2[b][:].rearrange("p (a b) -> p a b", b=128)), deps=[trl],
            tag=tag + "ev")
        Rptr2[pb_].read(tev)
        return tev

    last_out_dma = None
    prev_chunk_done = list(CONV_DONE)
    prev_b0_free = []
    TAPREG = {}

    def _reg_taps(**kw):
        TAPREG.update(kw)

    _reg_taps(B0=(B0, F32), B1=(B1, F32), HT=(HT, BF16), QT=(QT, BF16),
              KT=(KT, BF16), VT=(VT, BF16), GT=(GT, BF16), HT2=(HT2, BF16),
              STATS=(STATS, F32), SMX=(SMX, F32), OTB=(OTB, BF16),
              PMASK=(PMASK, BF16), TMS1=(TMS1, BF16), FFACC=(FFACC, F32),
              PT=(PT, BF16), DIAG=(DIAG, BF16),
              ROWB=(ROWB, BF16), BF1C=(BF1C, F32), WF2SB=(WF2SB, BF16),
              WDSB=(WDSB, BF16))

    def chunk_loop():
        nonlocal last_out_dma, prev_chunk_done
        phase_end("conv", -1, CONV_DONE)
        for ch in range(dev_nch):
            chunk_body(ch)

    def chunk_body(ch):
        nonlocal last_out_dma, prev_chunk_done
        ch_half = ch % 2
        new_seq = (ch_half == 0)
        CHB = [prev_chunk_done]

        x_toks = []
        for t in range(NT):
            dep = prev_b0_free[t] if prev_b0_free else CHB
            x_toks.append(P.dma("P", f"sx{t}", lambda e, tt=t: e.dma_start(
                B0[:, tt * 1024:(tt + 1) * 1024],
                xs[ch * 1024 + tt * 128:ch * 1024 + (tt + 1) * 128, :]),
                deps=dep, tag="xload"))
        wq_toks, wk_toks, wv_toks = [], [], []
        for kc in range(KC):
            wq_toks.append(P.dma("Y", "swq", lambda e, k=kc: e.dma_start(
                WQSB[:, k * 1024:(k + 1) * 1024], s_wq[k * 128:(k + 1) * 128, :]),
                deps=CHB, tag="wqld"))
            wk_toks.append(P.dma("Y", "swq", lambda e, k=kc: e.dma_start(
                WKSB[:, k * 256:(k + 1) * 256], s_wk[k * 128:(k + 1) * 128, :]),
                deps=CHB, tag="wkld"))
            wv_toks.append(P.dma("Y", "swq", lambda e, k=kc: e.dma_start(
                WVSB[:, k * 256:(k + 1) * 256], s_wv[k * 128:(k + 1) * 128, :]),
                deps=CHB, tag="wvld"))

        def table_load(qtr, extra):
            pos0 = ch_half * 1024 + qtr * 256
            ctv = C["costab"][pos0:pos0 + 256, :].rearrange(
                "(b p) d -> p b d", p=128)
            stv = C["sintab"][pos0:pos0 + 256, :].rearrange(
                "(b p) d -> p b d", p=128)
            tc = P.dma("P", "sct", lambda e, v=ctv: e.dma_start(
                CTB[:].rearrange("p (b d) -> p b d", d=1024), v),
                deps=CHB + extra, tag="ctld")
            ts_ = P.dma("P", "sct", lambda e, v=stv: e.dma_start(
                STB[:].rearrange("p (b d) -> p b d", d=1024), v),
                deps=CHB + extra, tag="stld")
            return [tc, ts_]

        tab_toks = table_load(0, [])

        halo_r = []
        if not new_seq:
            ktv = KT[:].rearrange("p (j a) -> p j a", a=1280)
            t1_ = P.op("V", lambda e, v=ktv: e.tensor_copy(
                v[:, :, 0:256], KTH[:].rearrange("p (j a) -> p j a", a=256)),
                deps=CHB, tag="halork")
            t2_ = P.op("V", lambda e: e.tensor_copy(VT[:, 0:512], VTH[:]),
                       deps=CHB, tag="halorv")
            halo_r = [t1_, t2_]

        ht_toks = []
        for half in range(2):
            lntoks = []
            for t in range(half * 4, half * 4 + 4):
                lntoks += ln_stat_tile(B0, t, DMP1, [x_toks[t]],
                                       half == 0 and t == 0)
            trv, _ = ln_stats_finish(lntoks, lo=half * 4, hi=half * 4 + 4)
            for t in range(half * 4, half * 4 + 4):
                sl = t % 2
                tap_ = P.op("V", lambda e, tt=t, s=sl: e.tensor_scalar(
                    out=TMS1[:, s * 1024:(s + 1) * 1024],
                    in0=B0[:, tt * 1024:(tt + 1) * 1024],
                    scalar1=STATS[:, 16 + tt:17 + tt],
                    scalar2=STATS[:, 40 + tt:41 + tt],
                    op0=ALU.subtract, op1=ALU.mult),
                    deps=[trv, x_toks[t]] + RTMS[sl].wdep(), tag="ln1ap")
                RSTATS.read(tap_)
                RTMS[sl].write(tap_)
                tfl = flip_tile(TMS1, sl * 1024, t, HT,
                                [tap_] + (RHT.wdep() if t == 0 else []), "h1")
                RTMS[sl].read(tfl)
                ht_toks.append(tfl)
        RHT.write(ht_toks[-1])
        phase_end("a0", ch, ht_toks)

        kt_toks, vt_toks, qt_toks = [], [], []
        last_rope = []
        for t in range(NT):
            if t in (2, 4, 6):
                tab_toks = table_load(t // 2, last_rope)
            tb = t % 2
            qb = [pbk[0], pbk[1]] if t % 2 == 0 else [pbk[2], pbk[3]]
            qres = [Rpb[0], Rpb[1]] if t % 2 == 0 else [Rpb[2], Rpb[3]]
            qmm = [None, None]
            for s in range(2):
                rv = ROWB[0:1, s * 512:(s + 1) * 512].rearrange(
                    "a (e p j) -> a p e j", e=2, p=4, j=64)
                rvl = ROWL[0:1, s * 512:(s + 1) * 512].rearrange(
                    "a (e p j) -> a p e j", e=2, p=4, j=64)
                qmm[s] = P.op("T", lambda e, ss=s, q=qb, r=rv: e.matmul(
                    q[ss][:], ONE1[:], r, start=True, stop=False),
                    deps=ROWS_DONE + qres[s].wdep(), tag="qseed")
                qmm[s] = P.op("T", lambda e, ss=s, q=qb, r=rvl: e.matmul(
                    q[ss][:], ONE1[:], r, start=False, stop=False),
                    deps=[], tag="qseedlo")
            vmm = P.op("T", lambda e: e.matmul(
                pbk[4][:, 0:256], ONE1[:], ROWB[0:1, 3328:3584],
                start=True, stop=False), deps=ROWS_DONE + Rpb[4].wdep(),
                tag="vseed")
            vmm = P.op("T", lambda e: e.matmul(
                pbk[4][:, 0:256], ONE1[:], ROWL[0:1, 3328:3584],
                start=False, stop=False), deps=[], tag="vseedlo")
            for kc in range(KC):
                for s in range(2):
                    rv = WQSB[:, kc * 1024 + s * 512:kc * 1024 + (s + 1) * 512]
                    rv = rv.rearrange("a (e p j) -> a p e j", e=2, p=4, j=64)
                    qmm[s] = P.op("T", lambda e, k=kc, ss=s, q=qb, tt=t, r=rv:
                                  e.matmul(
                        q[ss][:],
                        HT[:, k * 1024 + tt * 128:k * 1024 + (tt + 1) * 128],
                        r, start=False, stop=(k == KC - 1)),
                        deps=[ht_toks[t], wv_toks[-1]], tag="qmm")
                vmm = P.op("T", lambda e, k=kc, tt=t: e.matmul(
                    pbk[4][:, 0:256],
                    HT[:, k * 1024 + tt * 128:k * 1024 + (tt + 1) * 128],
                    WVSB[:, k * 256:(k + 1) * 256], start=False,
                    stop=(k == KC - 1)),
                    deps=[ht_toks[t], wv_toks[-1]], tag="vmm")
            for s in range(2):
                qres[s].write(qmm[s])
            Rpb[4].write(vmm)
            tvv = P.op("V", lambda e, tt=t: e.tensor_copy(
                VT[:, (tt + 2) * 256:(tt + 3) * 256], pbk[4][:, 0:256]),
                deps=[vmm] + halo_r + (RVT.wdep() if t == 0 else []), tag="vev")
            Rpb[4].read(tvv)
            vt_toks.append(tvv)
            kmm = P.op("T", lambda e: e.matmul(
                pbk[4][:, 0:256], ONE1[:], ROWB[0:1, 3072:3328],
                start=True, stop=False), deps=ROWS_DONE + [tvv], tag="kseed")
            kmm = P.op("T", lambda e: e.matmul(
                pbk[4][:, 0:256], ONE1[:], ROWL[0:1, 3072:3328],
                start=False, stop=False), deps=[], tag="kseedlo")
            for kc in range(KC):
                kmm = P.op("T", lambda e, k=kc, tt=t: e.matmul(
                    pbk[4][:, 0:256],
                    HT[:, k * 1024 + tt * 128:k * 1024 + (tt + 1) * 128],
                    WKSB[:, k * 256:(k + 1) * 256], start=False,
                    stop=(k == KC - 1)),
                    deps=[ht_toks[t], wv_toks[-1]], tag="kmm")
            Rpb[4].write(kmm)
            sl = t % 2
            rsdep = RRS.wdep()
            ropedeps = []
            for s in range(2):
                ct = CTB[:, tb * 1024 + s * 512:tb * 1024 + (s + 1) * 512]
                st_ = STB[:, tb * 1024 + s * 512:tb * 1024 + (s + 1) * 512]
                t1_ = P.op("V", lambda e, ss=s, q=qb, c=ct: e.tensor_tensor(
                    out=RS[:, ss * 512:(ss + 1) * 512], in0=q[ss][:], in1=c,
                    op=ALU.mult), deps=[qmm[s]] + tab_toks + rsdep, tag="ropet1")
                qv = qb[s][:].rearrange("p (a hh j) -> p a hh j", hh=2, j=32)
                sv = st_.rearrange("p (a hh j) -> p a hh j", hh=2, j=32)
                uv = RS[:, 1024 + s * 512:1024 + (s + 1) * 512].rearrange(
                    "p (a hh j) -> p a hh j", hh=2, j=32)
                tua = P.op("V", lambda e, q=qv, s_=sv, u=uv: e.tensor_tensor(
                    out=u[:, :, 0, :], in0=q[:, :, 1, :], in1=s_[:, :, 0, :],
                    op=ALU.mult), deps=[qmm[s]] + tab_toks + rsdep, tag="ropeua")
                tub = P.op("V", lambda e, q=qv, s_=sv, u=uv: e.tensor_tensor(
                    out=u[:, :, 1, :], in0=q[:, :, 0, :], in1=s_[:, :, 1, :],
                    op=ALU.mult), deps=[qmm[s]] + tab_toks + rsdep, tag="ropeub")
                tadd = P.op("V", lambda e, ss=s, slot=sl: e.tensor_tensor(
                    out=TMS1[:, slot * 1024 + ss * 512:slot * 1024 + (ss + 1) * 512],
                    in0=RS[:, ss * 512:(ss + 1) * 512],
                    in1=RS[:, 1024 + ss * 512:1024 + (ss + 1) * 512],
                    op=ALU.add),
                    deps=[t1_, tua, tub] + (RTMS[sl].wdep() if s == 0 else []),
                    tag="ropeadd")
                qres[s].read(tadd)
                ropedeps.append(tadd)
            RRS.write(ropedeps[-1])
            RTMS[sl].write(ropedeps[-1])
            tfl = flip_tile(TMS1, sl * 1024, t, QT,
                            ropedeps + (RQT.wdep() if t == 0 else []), "qf")
            RTMS[sl].read(tfl)
            qt_toks.append(tfl)
            ct = CTB[:, tb * 1024:tb * 1024 + 256]
            st_ = STB[:, tb * 1024:tb * 1024 + 256]
            rsdep2 = [ropedeps[-1]]
            t1_ = P.op("V", lambda e, c=ct: e.tensor_tensor(
                out=RS[:, 0:256], in0=pbk[4][:, 0:256], in1=c, op=ALU.mult),
                deps=[kmm] + tab_toks + rsdep2, tag="kropet1")
            kv_ = pbk[4][:, 0:256].rearrange("p (a hh j) -> p a hh j", hh=2, j=32)
            sv = st_.rearrange("p (a hh j) -> p a hh j", hh=2, j=32)
            uv = RS[:, 256:512].rearrange("p (a hh j) -> p a hh j", hh=2, j=32)
            tua = P.op("V", lambda e, q=kv_, s_=sv, u=uv: e.tensor_tensor(
                out=u[:, :, 0, :], in0=q[:, :, 1, :], in1=s_[:, :, 0, :],
                op=ALU.mult), deps=[kmm] + tab_toks + rsdep2, tag="kropeua")
            tub = P.op("V", lambda e, q=kv_, s_=sv, u=uv: e.tensor_tensor(
                out=u[:, :, 1, :], in0=q[:, :, 0, :], in1=s_[:, :, 1, :],
                op=ALU.mult), deps=[kmm] + tab_toks + rsdep2, tag="kropeub")
            ksl = t % 2
            tadd = P.op("V", lambda e, s=ksl: e.tensor_tensor(
                out=KTM[:, s * 256:(s + 1) * 256], in0=RS[:, 0:256],
                in1=RS[:, 256:512], op=ALU.add),
                deps=[t1_, tua, tub] + RKTM[ksl].wdep(), tag="kropeadd")
            RRS.write(tadd)
            RKTM[ksl].write(tadd)
            Rpb[4].read(tadd)
            last_rope = [tadd]
            trl = None
            for j in range(2):
                dep = [tadd]
                if j == 0:
                    dep += Rptr2[1].wdep()
                trl = P.op("T", lambda e, jj=j, s=ksl: e.transpose(
                    ptr2[1][:, jj * 128:(jj + 1) * 128],
                    KTM[:, s * 256 + jj * 128:s * 256 + (jj + 1) * 128], IDB[:]),
                    deps=dep, tag="kftr")
            Rptr2[1].write(trl)
            ktv2 = KT[:].rearrange("p (j a) -> p j a", a=1280)
            tev = P.op("V", lambda e, tt=t, v=ktv2: e.tensor_copy(
                v[:, :, (tt + 2) * 128:(tt + 3) * 128],
                ptr2[1][:, 0:256].rearrange("p (a b) -> p a b", b=128)),
                deps=[trl] + halo_r + (RKT.wdep() if t == 0 else []), tag="kfev")
            Rptr2[1].read(tev)
            RKTM[ksl].read(tev)
            kt_toks.append(tev)
        RQT.write(qt_toks[-1])
        RKT.write(kt_toks[-1])
        RVT.write(vt_toks[-1])
        a1_tail = [qt_toks[-1], kt_toks[-1], vt_toks[-1]]
        phase_end("a1", ch, qt_toks + kt_toks + vt_toks)

        a3_toks, u_toks = [], []
        ln2_stat_toks = []
        last_tex = [None] * H
        last_recip = [None, None]
        last_norm = [[], []]
        wo_toks = []
        def a2_stage12(qt, h0, h1, st=None):
            nonlocal last_tex
            if new_seq:
                nk = min(qt + 1, 3)
            else:
                nk = 3
            tidx0 = qt + (3 - nk)
            band = {3: BAND3, 2: BAND2, 1: BAND1}[nk]
            w = nk * 128
            if st is None:
                st = dict(qt=qt, nk=nk, tidx0=tidx0, w=w,
                          stt_toks=[None] * H, av_last=[None] * H,
                          tex_l=[None] * H)
            stt_toks = st["stt_toks"]
            av_last = st["av_last"]
            tex_l = st["tex_l"]
            par = qt % 2
            rs_base = 16 + 16 * par
            rc_base = 48 + 16 * par
            for h in range(h0, h1):
                bank = h % 3
                kvh = h // 4
                qb_ = (h % 4) * 2 + (h // 4) % 2 + (h // 8) * 8
                lhs = QT[(qb_ % 2) * 64:(qb_ % 2) * 64 + 64,
                         (qb_ // 2) * 1024 + qt * 128:(qb_ // 2) * 1024 + (qt + 1) * 128]
                rhs = KT[(kvh % 2) * 64:(kvh % 2) * 64 + 64,
                         (kvh // 2) * 1280 + tidx0 * 128:
                         (kvh // 2) * 1280 + tidx0 * 128 + w]
                tsm = P.op("T", lambda e, bk=bank, l=lhs, r=rhs, ww=w: e.matmul(
                    pbk[bk][:, 0:ww], l, r, start=True, stop=True),
                    deps=[qt_toks[qt], kt_toks[qt]] + halo_r + Rpb[bank].wdep(),
                    tag="smm")
                Rpb[bank].write(tsm)
                tmx = P.op("V", lambda e, bk=bank, hh=h, ww=w: e.reduce_max(
                    SMX[:, hh:hh + 1], pbk[bk][:, 0:ww], axis=AX.X, negate=True),
                    deps=[tsm, last_tex[h]], tag="smax")
                Rpb[bank].read(tmx)
                tex = P.op("S", lambda e, bk=bank, hh=h, ww=w: e.activation(
                    PMASK[:, hh * 384:hh * 384 + ww], pbk[bk][:, 0:ww], AF.Exp,
                    bias=SMX[:, hh:hh + 1], scale=1.0),
                    deps=[tmx] + RPM[h].wdep(), tag="sexp")
                Rpb[bank].read(tex)
                RPM[h].write(tex)
                tex_l[h] = tex
                last_tex[h] = tex
            for h in range(h0, h1):
                tst = P.op("V", lambda e, hh=h, ww=w, bd=band, rb=rs_base:
                           e.scalar_tensor_tensor(
                    out=PMASK[:, hh * 384:hh * 384 + ww],
                    in0=PMASK[:, hh * 384:hh * 384 + ww], scalar=1.0,
                    in1=bd[:, 0:ww], op0=ALU.mult, op1=ALU.mult,
                    accum_out=SMX[:, rb + hh:rb + hh + 1]),
                    deps=[tex_l[h], last_recip[par]], tag="sstt")
                RPM[h].write(tst)
                stt_toks[h] = tst
            return st

        def a2_stage3(st, g):
            nonlocal a3_toks, u_toks, ln2_stat_toks, wo_toks, last_recip, \
                last_norm
            qt = st["qt"]; nk = st["nk"]; tidx0 = st["tidx0"]; w = st["w"]
            stt_toks = st["stt_toks"]; av_last = st["av_last"]
            par = qt % 2
            rs_base = 16 + 16 * par
            rc_base = 48 + 16 * par
            osl = qt % 2
            ot_toks = st.setdefault("ot_toks", [])
            if True:
                trec = P.op("V", lambda e, gg=g, rb=rs_base, rc=rc_base:
                            e.reciprocal(
                    SMX[:, rc + gg * 8:rc + gg * 8 + 8],
                    SMX[:, rb + gg * 8:rb + gg * 8 + 8]),
                    deps=stt_toks[g * 8:(g + 1) * 8] + last_norm[par],
                    tag="srecip")
                if g == 1:
                    last_recip[par] = trec
                    last_norm[par] = []
                av_grp = []
                for h in range(g * 8, (g + 1) * 8):
                    kvh = h // 4
                    tnm = P.op("V", lambda e, hh=h, ww=w, rc=rc_base:
                               e.tensor_scalar_mul(
                        PMASK[:, hh * 384:hh * 384 + ww],
                        PMASK[:, hh * 384:hh * 384 + ww],
                        SMX[:, rc + hh:rc + hh + 1]),
                        deps=[trec, stt_toks[h]], tag="snorm")
                    RPM[h].write(tnm)
                    last_norm[par].append(tnm)
                    ppar = h % 2
                    psl = h % 4
                    trl = None
                    for i in range(nk):
                        dep = [tnm]
                        if i == 0:
                            dep += Rptr2[ppar].wdep()
                        trl = P.op("T", lambda e, hh=h, ii=i, pp=ppar:
                                   e.transpose(
                            ptr2[pp][:, ii * 128:(ii + 1) * 128],
                            PMASK[:, hh * 384 + ii * 128:hh * 384 + (ii + 1) * 128],
                            IDB[:]), deps=dep, tag="ptr")
                    Rptr2[ppar].write(trl)
                    RPM[h].read(trl)
                    tevp = P.op("S", lambda e, pp=ppar, ps=psl, ww=w:
                                e.activation(
                        PT[:, ps * 384:ps * 384 + ww],
                        ptr2[pp][:, 0:ww], AF.Copy),
                        deps=[trl] + RPT[psl].wdep(), tag="ptev")
                    Rptr2[ppar].read(tevp)
                    RPT[psl].write(tevp)
                    avl = None
                    for i in range(nk):
                        dep = [tevp, vt_toks[qt]] + halo_r
                        if i == 0 and h % 2 == 0:
                            dep += Raux.wdep()
                        avl = P.op("T", lambda e, ii=i, hh=h, k=kvh,
                                   t0=tidx0, ps=psl: e.matmul(
                            paux[(hh % 2) * 64:(hh % 2) * 64 + 64,
                                 ((hh // 2) % 4) * 128:((hh // 2) % 4 + 1) * 128],
                            VT[:, (t0 + ii) * 256 + k * 64:(t0 + ii) * 256 + k * 64 + 64],
                            PT[:, ps * 384 + ii * 128:ps * 384 + (ii + 1) * 128],
                            start=(ii == 0), stop=(ii == nk - 1)), deps=dep,
                            tag="avmm")
                    RPT[psl].read(avl)
                    av_last[h] = avl
                    av_grp.append(avl)
                    if h % 2 == 0:
                        Raux.write(avl)
                tev = P.op("S", lambda e, gg=g, o=osl: e.activation(
                    OTB[:, o * 1024 + gg * 512:o * 1024 + (gg + 1) * 512],
                    paux[:, 0:512], AF.Copy),
                    deps=av_grp + (ROT[osl].wdep() if g == 0 else []), tag="avev")
                Raux.read(tev)
                ot_toks.append(tev)
            if g == 0:
                return
            if qt == 0:
                for kc in range(KC):
                    wo_toks.append(P.dma("Y", "swo", lambda e, k=kc: e.dma_start(
                        WOSB[:, k * 1024:(k + 1) * 1024],
                        s_wo[k * 128:(k + 1) * 128, :]),
                        deps=[qt_toks[-1], qmm[0], qmm[1]] + cv_wo, tag="wold"))
            ev_toks = []
            for s in range(2):
                wb = 3 + s
                wos = P.op("T", lambda e, ss=s, b=wb: e.matmul(
                    pbk[b][:], ONE1[:],
                    ROWB[0:1, 1024 + ss * 512:1024 + (ss + 1) * 512],
                    start=True, stop=False),
                    deps=ROWS_DONE + Rpb[wb].wdep(), tag="woseed")
                wol = P.op("T", lambda e, ss=s, b=wb: e.matmul(
                    pbk[b][:], ONE1[:],
                    ROWL[0:1, 1024 + ss * 512:1024 + (ss + 1) * 512],
                    start=False, stop=False), deps=[], tag="woseedlo")
                for p in range(KC):
                    wol = P.op("T", lambda e, pp=p, ss=s, o=osl, b=wb: e.matmul(
                        pbk[b][:],
                        OTB[:, o * 1024 + pp * 128:o * 1024 + (pp + 1) * 128],
                        WOSB[:, pp * 1024 + ss * 512:pp * 1024 + (ss + 1) * 512],
                        start=False, stop=(pp == KC - 1)),
                        deps=[ot_toks[-1], wo_toks[-1]], tag="womm")
                Rpb[wb].write(wol)
                te = P.op("V", lambda e, ss=s, q=qt, b=wb: e.tensor_tensor(
                    out=B1[:, q * 1024 + ss * 512:q * 1024 + (ss + 1) * 512],
                    in0=pbk[b][:],
                    in1=B0[:, q * 1024 + ss * 512:q * 1024 + (ss + 1) * 512],
                    op=ALU.add), deps=[wol], tag="r1ev")
                Rpb[wb].read(te)
                ev_toks.append(te)
            ROT[osl].read(ev_toks[-1])
            tu = P.op("V", lambda e, q=qt: e.tensor_tensor(
                out=B0[:, q * 1024:(q + 1) * 1024],
                in0=B0[:, q * 1024:(q + 1) * 1024],
                in1=B1[:, q * 1024:(q + 1) * 1024], op=ALU.add),
                deps=ev_toks, tag="uev")
            a3_toks += ev_toks
            u_toks.append(tu)
            ln2_stat_toks += ln_stat_tile(B1, qt, DMP1, ev_toks, qt == 0)

        phase_end("a3", ch, u_toks + a3_toks)


        st_prev = a2_stage12(0, 0, 8)
        a2_stage12(0, 8, 16, st_prev)
        for qt in range(1, NT):
            a2_stage3(st_prev, 0)
            st_cur = a2_stage12(qt, 0, 8)
            a2_stage3(st_prev, 1)
            a2_stage12(qt, 8, 16, st_cur)
            st_prev = st_cur
        a2_stage3(st_prev, 0)
        a2_stage3(st_prev, 1)

        halo_s = []
        if ch_half == 0:
            ktv2 = KT[:].rearrange("p (j a) -> p j a", a=1280)
            t1_ = P.op("V", lambda e, v=ktv2: e.tensor_copy(
                KTH[:].rearrange("p (j a) -> p j a", a=256), v[:, :, 1024:1280]),
                deps=[kt_toks[-1], kt_toks[-2]], tag="halosk")
            t2_ = P.op("V", lambda e: e.tensor_copy(VTH[:], VT[:, 2048:2560]),
                       deps=[vt_toks[-1], vt_toks[-2]], tag="halosv")
            halo_s = [t1_, t2_]
            RKTH.write(t1_)
            RVTH.write(t2_)

        ht2_toks = []
        for half in range(2):
            trv2, _ = ln_stats_finish(ln2_stat_toks[half * 8:half * 8 + 8],
                                      lo=half * 4, hi=half * 4 + 4)
            for t in range(half * 4, half * 4 + 4):
                sl = t % 2
                tap_ = P.op("V", lambda e, tt=t, s=sl: e.tensor_scalar(
                    out=TMS1[:, s * 1024:(s + 1) * 1024],
                    in0=B1[:, tt * 1024:(tt + 1) * 1024],
                    scalar1=STATS[:, 16 + tt:17 + tt],
                    scalar2=STATS[:, 40 + tt:41 + tt],
                    op0=ALU.subtract, op1=ALU.mult),
                    deps=[trv2] + RTMS[sl].wdep(), tag="ln2ap")
                RSTATS.read(tap_)
                RTMS[sl].write(tap_)
                tfl = flip_tile(TMS1, sl * 1024, t, HT,
                                [tap_] + (RHT.wdep() if t == 0 else []), "h2")
                RTMS[sl].read(tfl)
                ht2_toks.append(tfl)
        RHT.write(ht2_toks[-1])

        phase_end("ln2", ch, ht2_toks)
        g1_done = (ht2_toks + halo_s + u_toks +
                   [qt_toks[-1], kt_toks[-1], vt_toks[-1]])

        ff_evac_toks = []
        ln3_stat_toks = []
        half_tail = []
        for hf in range(2):
            wf2_toks = []
            for mi in range(16):
                m = hf * 16 + mi
                wf2_toks.append(P.dma("Y", "swf2", lambda e, mm_=m, mi_=mi:
                                      e.dma_start(
                    WF2SB[:, mi_ * 1024:(mi_ + 1) * 1024],
                    s_wf2[mm_ * 128:(mm_ + 1) * 128, :]),
                    deps=(a1_tail if hf == 0 else half_tail) + cv_wf2,
                    tag="wf2ld"))
            gt_toks = []
            for mi in range(16):
                m = hf * 16 + mi
                psl = mi % 2
                tpl = P.dma("Y", "swf1a" if psl == 0 else "swf1b",
                            lambda e, mm_=m, p=psl: e.dma_start(
                                PAN1[:, p * 1024:(p + 1) * 1024], s_wf1p[mm_]),
                            deps=RPAN1[psl].wdep() + cv_wf1 +
                                 ([ht2_toks[-1]] if hf == 0 and mi < 2 else []),
                            tag="wf1pld")
                RPAN1[psl].write(tpl)
                bsel = [Rpb[0], Rpb[1]] if mi % 2 == 0 else [Rpb[2], Rpb[3]]
                bbk = [pbk[0], pbk[1]] if mi % 2 == 0 else [pbk[2], pbk[3]]
                for s in range(2):
                    mm_l = None
                    for kc in range(KC):
                        dep = [tpl, ht2_toks[-1]]
                        if kc == 0:
                            dep += bsel[s].wdep()
                        mm_l = P.op("T", lambda e, k=kc, ss=s, p=psl, b=bbk:
                                    e.matmul(
                            b[ss][:],
                            PAN1[:, p * 1024 + k * 128:p * 1024 + (k + 1) * 128],
                            HT[:, k * 1024 + ss * 512:k * 1024 + (ss + 1) * 512],
                            start=(k == 0), stop=(k == KC - 1)), deps=dep,
                            tag="f1amm")
                    bsel[s].write(mm_l)
                    RPAN1[psl].read(mm_l)
                    tg = P.op("S", lambda e, ss=s, mi_=mi, m_=m, b=bbk:
                              e.activation(
                        GT[:, mi_ * 1024 + ss * 512:mi_ * 1024 + (ss + 1) * 512],
                        b[ss][:], AF.Gelu, bias=BF1C[:, m_:m_ + 1], scale=1.0),
                        deps=[mm_l, tbf1c] +
                             (RGT.wdep() if mi == 0 and s == 0 else []),
                        tag="gelu")
                    bsel[s].read(tg)
                    gt_toks.append(tg)
            RGT.write(gt_toks[-1])
            for t in range(NT):
                bsel = [Rpb[0], Rpb[1]] if t % 2 == 0 else [Rpb[2], Rpb[3]]
                bbk = [pbk[0], pbk[1]] if t % 2 == 0 else [pbk[2], pbk[3]]
                mm_l = [None, None]
                if hf == 0:
                    for s in range(2):
                        mm_l[s] = P.op("T", lambda e, ss=s, b=bbk: e.matmul(
                            b[ss][:], ONE1[:],
                            ROWB[0:1, 2048 + ss * 512:2048 + (ss + 1) * 512],
                            start=True, stop=False),
                            deps=ROWS_DONE + bsel[s].wdep(), tag="f2seed")
                        mm_l[s] = P.op("T", lambda e, ss=s, b=bbk: e.matmul(
                            b[ss][:], ONE1[:],
                            ROWL[0:1, 2048 + ss * 512:2048 + (ss + 1) * 512],
                            start=False, stop=False), deps=[], tag="f2seedlo")
                for mi in range(16):
                    for s in range(2):
                        dep = [gt_toks[-1], wf2_toks[-1]]
                        if mm_l[s] is None:
                            dep += bsel[s].wdep()
                        first = (mm_l[s] is None)
                        mm_l[s] = P.op("T", lambda e, mi_=mi, ss=s, tt=t, b=bbk,
                                       f=first: e.matmul(
                            b[ss][:],
                            GT[:, mi_ * 1024 + tt * 128:mi_ * 1024 + (tt + 1) * 128],
                            WF2SB[:, mi_ * 1024 + ss * 512:mi_ * 1024 + (ss + 1) * 512],
                            start=f, stop=(mi_ == 15)), deps=dep, tag="f1bmm")
                for s in range(2):
                    bsel[s].write(mm_l[s])
                    if hf == 0:
                        te = P.op("V", lambda e, tt=t, ss=s, b=bbk:
                                  e.scalar_tensor_tensor(
                            out=FFACC[:, tt * 1024 + ss * 512:tt * 1024 + (ss + 1) * 512],
                            in0=b[ss][:], scalar=1.0,
                            in1=B0[:, tt * 1024 + ss * 512:tt * 1024 + (ss + 1) * 512],
                            op0=ALU.mult, op1=ALU.add),
                            deps=[mm_l[s], u_toks[t]] +
                                 (RFF.wdep() if t == 0 and s == 0 else []),
                            tag="ffacc")
                    else:
                        te = P.op("V", lambda e, tt=t, ss=s, b=bbk:
                                  e.scalar_tensor_tensor(
                            out=B0[:, tt * 1024 + ss * 512:tt * 1024 + (ss + 1) * 512],
                            in0=b[ss][:], scalar=1.0,
                            in1=FFACC[:, tt * 1024 + ss * 512:tt * 1024 + (ss + 1) * 512],
                            op0=ALU.mult, op1=ALU.add),
                            deps=[mm_l[s], RFF.w, u_toks[t]], tag="preln")
                    bsel[s].read(te)
                    ff_evac_toks.append(te)
                if hf == 1:
                    ln3_stat_toks += ln_stat_tile(
                        B0, t, DMP2, ff_evac_toks[-2:], t == 0)
            if hf == 0:
                RFF.write(ff_evac_toks[-1])
                half_tail = ff_evac_toks[-16:]
        RB0.write(ff_evac_toks[-1])
        pre_toks = ff_evac_toks[16 * 1:][-16:]
        phase_end("ffn1", ch, ff_evac_toks)

        x1_toks, x1t_toks = [], []
        for half in range(2):
            trv3, tnm3 = ln_stats_finish(ln3_stat_toks[half * 8:half * 8 + 8],
                                         want_nmr=True, lo=half * 4,
                                         hi=half * 4 + 4)
            for t in range(half * 4, half * 4 + 4):
                tap_ = P.op("V", lambda e, tt=t: e.tensor_scalar(
                    out=B1[:, tt * 1024:(tt + 1) * 1024],
                    in0=B0[:, tt * 1024:(tt + 1) * 1024],
                    scalar1=STATS[:, 16 + tt:17 + tt],
                    scalar2=STATS[:, 40 + tt:41 + tt],
                    op0=ALU.subtract, op1=ALU.mult), deps=[trv3], tag="ln3v")
                RSTATS.read(tap_)
                x1_toks.append(tap_)
                sl = t % 2
                tap2 = P.op("S", lambda e, tt=t, s=sl: e.activation(
                    TMS2[:, s * 1024:(s + 1) * 1024],
                    B0[:, tt * 1024:(tt + 1) * 1024], AF.Identity,
                    bias=STATS[:, 48 + tt:49 + tt],
                    scale=STATS[:, 40 + tt:41 + tt]),
                    deps=[tnm3, RFF.w] + RTMS[sl].wdep(), tag="ln3s")
                RSTATS.read(tap2)
                RTMS[sl].write(tap2)
                tfl = flip_tile(TMS2, sl * 1024, t, HT,
                                [tap2, gt_toks[-1]] +
                                (RHT.wdep() if t == 0 else []), "x1f")
                RTMS[sl].read(tfl)
                x1t_toks.append(tfl)
        RB1.write(x1_toks[-1])
        RHT.write(x1t_toks[-1])
        phase_end("ln3", ch, x1_toks + x1t_toks)

        wd_toks = []
        h2_toks = []
        for mi in range(M2):
            wd_toks.append(P.dma("Y", "swd", lambda e, k=mi: e.dma_start(
                WDSB[:, k * 1024:(k + 1) * 1024],
                s_wdt[k * 128:(k + 1) * 128, :]),
                deps=(x1t_toks if mi == 0 else []) + cv_wd, tag="wdld"))
            pg, pu = mi % 2, 2 + mi % 2
            tpg = P.dma("Y", "swg" + str(mi % 2), lambda e, m=mi, p=pg:
                        e.dma_start(PAN2[:, p * 1024:(p + 1) * 1024], s_wgt[m]),
                        deps=(x1_toks + x1t_toks if mi < 2 else []) +
                             RPAN2[pg].wdep() + cv_wg, tag="wgld")
            tpu = P.dma("Y", "swu" + str(mi % 2), lambda e, m=mi, p=pu:
                        e.dma_start(PAN2[:, p * 1024:(p + 1) * 1024], s_wut[m]),
                        deps=(x1_toks + x1t_toks if mi < 2 else []) +
                             RPAN2[pu].wdep() + cv_wu, tag="wuld")
            RPAN2[pg].write(tpg)
            RPAN2[pu].write(tpu)
            for s in range(2):
                bg = ((2 * mi + s) % 2) * 2
                bu = bg + 1
                gmm, umm = None, None
                for kc in range(KC):
                    depg = [tpg, x1t_toks[-1]]
                    if kc == 0:
                        depg += Rpb[bg].wdep()
                    gmm = P.op("T", lambda e, k=kc, ss=s, p=pg, b=bg: e.matmul(
                        pbk[b][:],
                        PAN2[:, p * 1024 + k * 128:p * 1024 + (k + 1) * 128],
                        HT[:, k * 1024 + ss * 512:k * 1024 + (ss + 1) * 512],
                        start=(k == 0), stop=(k == KC - 1)), deps=depg,
                        tag="gmm")
                    depu = [tpu, x1t_toks[-1]]
                    if kc == 0:
                        depu += Rpb[bu].wdep()
                    umm = P.op("T", lambda e, k=kc, ss=s, p=pu, b=bu: e.matmul(
                        pbk[b][:],
                        PAN2[:, p * 1024 + k * 128:p * 1024 + (k + 1) * 128],
                        HT[:, k * 1024 + ss * 512:k * 1024 + (ss + 1) * 512],
                        start=(k == 0), stop=(k == KC - 1)), deps=depu,
                        tag="umm")
                Rpb[bg].write(gmm)
                Rpb[bu].write(umm)
                RPAN2[pg].read(gmm)
                RPAN2[pu].read(umm)
                ssl = s
                tsi = P.op("S", lambda e, b=bg, sl_=ssl: e.activation(
                    SIL[:, sl_ * 512:(sl_ + 1) * 512], pbk[b][:], AF.Silu,
                    scale=float(QSC)), deps=[gmm] + RSIL[ssl].wdep(), tag="silu")
                Rpb[bg].read(tsi)
                RSIL[ssl].write(tsi)
                th = P.op("V", lambda e, b=bu, sl_=ssl, m=mi, ss=s:
                          e.tensor_tensor(
                    out=HT2[:, m * 1024 + ss * 512:m * 1024 + (ss + 1) * 512],
                    in0=SIL[:, sl_ * 512:(sl_ + 1) * 512], in1=pbk[b][:],
                    op=ALU.mult),
                    deps=[tsi, umm] +
                         (RHT2.wdep() if mi == 0 and s == 0 else []), tag="hmul")
                Rpb[bu].read(th)
                RSIL[ssl].read(th)
                h2_toks.append(th)
        RHT2.write(h2_toks[-1])
        phase_end("ffn2a", ch, h2_toks)

        dn_toks = []
        ln4_stat_toks = []
        for t in range(NT):
            bsel = [Rpb[0], Rpb[1]] if t % 2 == 0 else [Rpb[2], Rpb[3]]
            bbk = [pbk[0], pbk[1]] if t % 2 == 0 else [pbk[2], pbk[3]]
            mm_l = [None, None]
            for mi in range(M2):
                for s in range(2):
                    dep = [h2_toks[-1], wd_toks[-1]]
                    if mm_l[s] is None:
                        dep += bsel[s].wdep()
                    mm_l[s] = P.op("T", lambda e, m=mi, ss=s, tt=t, b=bbk:
                                   e.matmul(
                        b[ss][:],
                        HT2[:, m * 1024 + tt * 128:m * 1024 + (tt + 1) * 128],
                        WDSB[:, m * 1024 + ss * 512:m * 1024 + (ss + 1) * 512],
                        start=(m == 0), stop=(m == M2 - 1)), deps=dep,
                        tag="dnmm")
            for s in range(2):
                bsel[s].write(mm_l[s])
                te = P.op("V", lambda e, tt=t, ss=s, b=bbk:
                          e.scalar_tensor_tensor(
                    out=B0[:, tt * 1024 + ss * 512:tt * 1024 + (ss + 1) * 512],
                    in0=b[ss][:], scalar=float(QSC * QSC),
                    in1=B1[:, tt * 1024 + ss * 512:tt * 1024 + (ss + 1) * 512],
                    op0=ALU.mult, op1=ALU.add),
                    deps=[mm_l[s], x1_toks[t]], tag="dnev")
                bsel[s].read(te)
                dn_toks.append(te)
            ln4_stat_toks += ln_stat_tile(B0, t, DMP4, dn_toks[-2:], t == 0)
        RB0.write(dn_toks[-1])
        phase_end("ffn2b", ch, dn_toks)

        out_toks = []
        b0_free = []
        for t in range(NT):
            if t % 4 == 0:
                trv4, _ = ln_stats_finish(
                    ln4_stat_toks[(t // 4) * 8:(t // 4) * 8 + 8],
                    lo=t, hi=t + 4)
            sl = t % 2
            war = [out_toks[-1]] if len(out_toks) >= 2 else (
                [out_toks[-1]] if out_toks else [])
            tap_ = P.op("V", lambda e, tt=t, s=sl: e.tensor_scalar(
                out=OS[:, s * 1024:(s + 1) * 1024],
                in0=B0[:, tt * 1024:(tt + 1) * 1024],
                scalar1=STATS[:, 16 + tt:17 + tt],
                scalar2=STATS[:, 40 + tt:41 + tt],
                op0=ALU.subtract, op1=ALU.mult),
                deps=[trv4, dn_toks[2 * t], dn_toks[2 * t + 1]] + war +
                     ROS[sl].rdep(), tag="ln4ap")
            RSTATS.read(tap_)
            ROS[sl].write(tap_)
            b0_free.append([tap_, ln4_stat_toks[2 * t + 1]])
            td = P.dma("S", "sy", lambda e, tt=t, s=sl: e.dma_start(
                y[ch * 1024 + tt * 128:ch * 1024 + (tt + 1) * 128, :],
                OS[:, s * 1024:(s + 1) * 1024]), deps=[tap_], tag="yout")
            out_toks.append(td)
            last_out_dma = td

        prev_chunk_done = dn_toks + out_toks + [trv4]
        prev_b0_free[:] = b0_free

    try:
        chunk_loop()
        stop_toks = [last_out_dma]
    except StopBuild:
        stop_toks = tap_specs[0]
    tap_dmas = []
    for name in DEV.get("taps", ()):
        h, dtp = TAPREG[name]
        cols = int(h.shape[1])
        tout = nc.dram_tensor("tap_" + name, [128, cols], dtp,
                              kind="ExternalOutput").ap()
        tap_dmas.append(P.dma("S", "stap",
                              lambda e, o=tout, hh=h: e.dma_start(o, hh[:]),
                              deps=stop_toks, tag="tap"))
    P.op("Y", lambda e: e.nop(), deps=stop_toks + tap_dmas, tag="final")
    P.build(nc)
    return nc


def kernel(**inputs):
    x = np.ascontiguousarray(inputs["L0_attn_x"], dtype=np.float32)
    consts = host_consts()
    base = {
        "wq": np.ascontiguousarray(inputs["L0_attn_Wq"], dtype=np.float32),
        "wk": np.ascontiguousarray(inputs["L0_attn_Wk"], dtype=np.float32),
        "wv": np.ascontiguousarray(inputs["L0_attn_Wv"], dtype=np.float32),
        "wo": np.ascontiguousarray(inputs["L0_attn_Wo"], dtype=np.float32),
        "wf1": np.ascontiguousarray(inputs["L0_attn_Wf1"], dtype=np.float32),
        "wf2": np.ascontiguousarray(inputs["L0_attn_Wf2"], dtype=np.float32),
        "wg": np.ascontiguousarray(inputs["L0_ffn_w_gate"], dtype=np.int32),
        "wu": np.ascontiguousarray(inputs["L0_ffn_w_up"], dtype=np.int32),
        "wd": np.ascontiguousarray(inputs["L0_ffn_w_down"], dtype=np.int32),
        "bq": np.ascontiguousarray(inputs["L0_attn_bq"].reshape(1, -1),
                                   dtype=np.float32),
        "bk": np.ascontiguousarray(inputs["L0_attn_bk"].reshape(1, -1),
                                   dtype=np.float32),
        "bv": np.ascontiguousarray(inputs["L0_attn_bv"].reshape(1, -1),
                                   dtype=np.float32),
        "bo": np.ascontiguousarray(inputs["L0_attn_bo"].reshape(1, -1),
                                   dtype=np.float32),
        "bf1": np.ascontiguousarray(inputs["L0_attn_bf1"].reshape(1, -1),
                                    dtype=np.float32),
        "bf2": np.ascontiguousarray(inputs["L0_attn_bf2"].reshape(1, -1),
                                    dtype=np.float32),
        "g1": np.ascontiguousarray(inputs["L0_attn_g1"].reshape(1, -1),
                                   dtype=np.float32),
        "b1": np.ascontiguousarray(inputs["L0_attn_b1"].reshape(1, -1),
                                   dtype=np.float32),
        "g2": np.ascontiguousarray(inputs["L0_attn_g2"].reshape(1, -1),
                                   dtype=np.float32),
        "b2": np.ascontiguousarray(inputs["L0_attn_b2"].reshape(1, -1),
                                   dtype=np.float32),
    }
    base.update(consts)
    nc = build_program()
    in_maps = []
    for c in range(NCORE):
        m = dict(base)
        m["xs"] = np.ascontiguousarray(x[2 * c:2 * c + 2].reshape(NCH * CH_T, D))
        in_maps.append(m)
    res = run_bass_kernel_spmd(nc, in_maps, core_ids=list(range(NCORE)))
    out = np.concatenate([r["y"].reshape(2, 2048, D) for r in res.results],
                         axis=0)
    return out


if __name__ == "__main__":
    nc = build_program()
    print("built ok; instructions:", nc.n_instructions())


# version 14 (final)
# speedup vs baseline: 1.0136x; 1.0136x over previous
import numpy as np
import ml_dtypes
import concourse.bass as bass
import concourse.mybir as mybir
from concourse.bass_utils import run_bass_kernel_spmd

F32 = mybir.dt.float32
BF16 = mybir.dt.bfloat16
I32 = mybir.dt.int32
AF = mybir.ActivationFunctionType
ALU = mybir.AluOpType
AX = mybir.AxisListType
BF = ml_dtypes.bfloat16

NCORE = 8
D = 1024
KC = 8
CH_T = 1024
NT = 8
NCH = 4
H = 16
HKV = 4
HD = 64
DKV = 256
DF1 = 4096
M1 = 32
DF2 = 3072
M2 = 24
EPS = 1e-5
QSC = 1.0 / 127.0


TAGMAP = {}


class Prog:
    def __init__(self):
        self.items = []

    def op(self, e, fn, deps=(), tag=""):
        return self._add(e, fn, deps, None, tag)

    def dma(self, q, sem, fn, deps=(), tag=""):
        assert q in ("Y", "S", "P")
        return self._add(q, fn, deps, sem, tag)

    def _add(self, e, fn, deps, dsem, tag):
        i = len(self.items)
        dd = set()
        for d in deps:
            if d is None:
                continue
            if isinstance(d, (list, tuple, set)):
                dd.update(x for x in d if x is not None)
            else:
                dd.add(d)
        self.items.append(dict(i=i, e=e, fn=fn, deps=sorted(dd), dsem=dsem,
                               tag=tag))
        return i

    def build(self, nc):
        targets = set()
        for it in self.items:
            targets.update(it["deps"])
        cnt = {}
        for it in self.items:
            s = it["dsem"] if it["dsem"] else ("E" + it["e"])
            it["sem"] = s
            if it["dsem"] is not None or (it["i"] in targets):
                amt = 16 if it["dsem"] else 1
                cnt[s] = cnt.get(s, 0) + amt
                it["inc"] = True
                it["val"] = cnt[s]
            else:
                it["inc"] = False
                it["val"] = None
        semh = {s: nc.alloc_semaphore(f"sm_{s}") for s in cnt}
        by_e = {}
        for it in self.items:
            by_e.setdefault(it["e"], []).append(it)
        idx = self.items

        def emit(eng, elist):
            water = {}
            for it in elist:
                for d in it["deps"]:
                    dit = idx[d]
                    assert dit["inc"], f"dep target {d} ({dit['tag']}) lacks inc"
                    s, v = dit["sem"], dit["val"]
                    if water.get(s, 0) < v:
                        eng.wait_ge(semh[s], v)
                        water[s] = v
                ins = it["fn"](eng)
                try:
                    TAGMAP[ins.ins.name] = (it["e"], it["tag"])
                except Exception:
                    try:
                        TAGMAP[ins.name] = (it["e"], it["tag"])
                    except Exception:
                        pass
                if it["inc"]:
                    ins.then_inc(semh[it["sem"]], 16 if it["dsem"] else 1)

        EMAP = dict(T="tensor", V="vector", S="scalar", P="gpsimd", Y="sync")
        with nc.Block() as blk:
            for e in ("T", "V", "S", "P", "Y"):
                if e in by_e:
                    getattr(blk, EMAP[e])(lambda eng, el=by_e[e]: emit(eng, el))
        return semh


class Res:
    __slots__ = ("w", "rs")

    def __init__(self):
        self.w = None
        self.rs = []

    def rdep(self):
        return [self.w] if self.w is not None else []

    def wdep(self):
        d = list(self.rs)
        if self.w is not None:
            d.append(self.w)
        return d

    def read(self, tok):
        self.rs.append(tok)
        return tok

    def write(self, tok):
        self.w = tok
        self.rs = []
        return tok


def host_consts():
    c = {}
    c["ident_bf"] = np.asarray(np.eye(128, dtype=np.float32), dtype=BF)
    c["ident_f32"] = np.eye(128, dtype=np.float32)
    c["ones1"] = np.asarray(np.ones((1, 128), dtype=np.float32), dtype=BF)
    half = HD // 2
    theta = 1.0 / 10000.0 ** (np.arange(half, dtype=np.float64) / half)
    ang = np.outer(np.arange(2048, dtype=np.float64), theta)
    cosv, sinv = np.cos(ang), np.sin(ang)
    ct = np.tile(np.concatenate([cosv, cosv], axis=1), (1, H))
    st = np.tile(np.concatenate([-sinv, sinv], axis=1), (1, H))
    c["costab"] = np.asarray(ct, dtype=np.float32)
    c["sintab"] = np.asarray(st, dtype=np.float32)
    r = np.arange(128)[:, None]
    cc = np.arange(384)[None, :]
    c["band3"] = np.asarray(((cc > r) & (cc <= r + 256)).astype(np.float32),
                            dtype=BF)
    c["band2"] = np.asarray((cc[:, :256] <= r + 128).astype(np.float32), dtype=BF)
    c["band1"] = np.asarray((cc[:, :128] <= r).astype(np.float32), dtype=BF)
    return c


DEV = {"nch": NCH, "stop": None, "taps": ()}


def build_program():
    nc = bass.Bass("TRN2", target_bir_lowering=False)
    P = Prog()
    dev_nch = DEV.get("nch", NCH)
    dev_stop = DEV.get("stop")
    tap_specs = []

    class StopBuild(Exception):
        pass

    def phase_end(name, ch, toks):
        if dev_stop == name and (name == "conv" or ch == dev_nch - 1):
            tap_specs.append(list(toks) if isinstance(toks, (list, tuple))
                             else [toks])
            raise StopBuild()

    xs = nc.dram_tensor("xs", [NCH * CH_T, D], F32, kind="ExternalInput").ap()
    y = nc.dram_tensor("y", [NCH * CH_T, D], F32, kind="ExternalOutput").ap()
    W = {}
    for nm, shp, dt in [
        ("wq", [D, D], F32), ("wk", [D, DKV], F32), ("wv", [D, DKV], F32),
        ("wo", [D, D], F32), ("wf1", [D, DF1], F32), ("wf2", [DF1, D], F32),
        ("wg", [DF2, D], I32), ("wu", [DF2, D], I32), ("wd", [D, DF2], I32),
        ("bq", [1, D], F32), ("bk", [1, DKV], F32), ("bv", [1, DKV], F32),
        ("bo", [1, D], F32), ("bf1", [1, DF1], F32), ("bf2", [1, D], F32),
        ("g1", [1, D], F32), ("b1", [1, D], F32), ("g2", [1, D], F32),
        ("b2", [1, D], F32),
    ]:
        W[nm] = nc.dram_tensor(nm, shp, dt, kind="ExternalInput").ap()
    C = {}
    for nm, shp in [("ident_bf", [128, 128]), ("ones1", [1, 128]),
                    ("band3", [128, 384]), ("band2", [128, 256]),
                    ("band1", [128, 128])]:
        C[nm] = nc.dram_tensor(nm, shp, BF16, kind="ExternalInput").ap()
    for nm in ("costab", "sintab"):
        C[nm] = nc.dram_tensor(nm, [2048, D], F32, kind="ExternalInput").ap()
    C["ident_f32"] = nc.dram_tensor("ident_f32", [128, 128], F32,
                                    kind="ExternalInput").ap()

    s_wq = nc.dram_tensor("s_wq", [D, D], BF16).ap()
    s_wk = nc.dram_tensor("s_wk", [D, DKV], BF16).ap()
    s_wv = nc.dram_tensor("s_wv", [D, DKV], BF16).ap()
    s_wo = nc.dram_tensor("s_wo", [D, D], BF16).ap()
    s_wf1p = nc.dram_tensor("s_wf1p", [M1, 128, D], BF16).ap()
    s_wf2 = nc.dram_tensor("s_wf2", [DF1, D], BF16).ap()
    s_wgt = nc.dram_tensor("s_wgt", [M2, 128, D], BF16).ap()
    s_wut = nc.dram_tensor("s_wut", [M2, 128, D], BF16).ap()
    s_wdt = nc.dram_tensor("s_wdt", [DF2, D], BF16).ap()

    BASE = 18560
    off = BASE

    def At(name, shape, dtype, o=None):
        nonlocal off
        esz = 2 if dtype == BF16 else 4
        nbytes = int(np.prod(shape[1:])) * esz
        if o is None:
            o = off
            off = (off + nbytes + 31) & ~31
        return nc.alloc_sbuf_tensor_at(name, list(shape), dtype, offset=o)

    B0 = At("B0", [128, NT * 1024], F32)
    B1 = At("B1", [128, NT * 1024], F32)
    HT = At("HT", [128, KC * 1024], BF16)
    STATS = At("STATS", [128, 64], F32)
    SMX = At("SMX", [128, 96], F32)
    BAND3 = At("BAND3", [128, 384], BF16)
    BAND2 = At("BAND2", [128, 256], BF16)
    BAND1 = At("BAND1", [128, 128], BF16)
    IDB = At("IDB", [128, 128], BF16)
    IDF = At("IDF", [128, 128], F32)
    ONE1 = At("ONE1", [1, 128], BF16)
    ROWB = At("ROWB", [1, 4096], BF16)
    ROWL = At("ROWL", [1, 4096], BF16)
    G1C = At("G1C", [128, 8], F32)
    B1C = At("B1C", [128, 8], F32)
    G2C = At("G2C", [128, 8], F32)
    B2C = At("B2C", [128, 8], F32)
    B1CB = At("B1CB", [128, 8], BF16)
    B2CB = At("B2CB", [128, 8], BF16)
    BF1C = At("BF1C", [128, 32], F32)
    KTH = At("KTH", [128, 512], BF16)
    VTH = At("VTH", [128, 512], BF16)
    ARENA = (off + 127) & ~127
    ARENA_SZ = 229344 - ARENA
    assert ARENA_SZ >= 102400, f"arena too small: {ARENA_SZ}"

    def Ar(name, shape, dtype, o):
        esz = 2 if dtype == BF16 else 4
        assert o + int(np.prod(shape[1:])) * esz <= ARENA_SZ, name
        return At(name, shape, dtype, o=ARENA + o)

    QT = Ar("QT", [128, KC * 1024], BF16, 0)
    CTB = Ar("CTB", [128, 2 * 1024], F32, 16384)
    STB = Ar("STB", [128, 2 * 1024], F32, 24576)
    WQSB = Ar("WQSB", [128, KC * 1024], BF16, 32768)
    WOSB = Ar("WOSB", [128, KC * 1024], BF16, 16384)
    WKSB = Ar("WKSB", [128, KC * 256], BF16, 49152)
    WVSB = Ar("WVSB", [128, KC * 256], BF16, 53248)
    RS = Ar("RS", [128, 2 * 1024], F32, 57344)
    KT = Ar("KT", [128, 2 * 10 * 128], BF16, 65536)
    VT = Ar("VT", [128, 10 * 256], BF16, 70656)
    PMASK = Ar("PMASK", [128, 16 * 384], BF16, 75776)
    PT = Ar("PT", [128, 4 * 384], BF16, 88064)
    OTB = Ar("OTB", [128, 2 * 1024], BF16, 91136)
    DIAG = Ar("DIAG", [128, 2 * 128], BF16, 95232)
    KTM = Ar("KTM", [128, 2 * 256], BF16, 95744)
    TMS1 = Ar("TMS1", [128, 2 * 1024], BF16, 96768)
    DMP1 = Ar("DMP1", [128, 1024], F32, 100864)
    WF2SB = Ar("WF2SB", [128, 16 * 1024], BF16, 32768)
    GT = Ar("GT", [128, 16 * 1024], BF16, 65536)
    PAN1 = Ar("PAN1", [128, 2 * 1024], BF16, 100864)
    FFACC = Ar("FFACC", [128, NT * 1024], F32, 0)
    TMS2 = Ar("TMS2", [128, 2 * 1024], BF16, 0)
    DMP2 = DMP1
    HT2 = Ar("HT2", [128, M2 * 1024], BF16, 0)
    WDSB = Ar("WDSB", [128, M2 * 1024], BF16, 49152)
    SIL = Ar("SIL", [128, 2 * 512], BF16, 98304)
    PAN2 = At("PAN2", [128, 4 * 1024], BF16, o=BASE)
    OS = Ar("OS", [128, 2 * 1024], F32, 8192)
    DMP4 = DMP1
    STG = Ar("STG", [128, KC * 4096], BF16, 0)
    CSA = Ar("CSA", [128, 4096], BF16, 65536)
    CSB = Ar("CSB", [128, 4096], BF16, 73728)
    CS2 = Ar("CS2", [128, 1024], BF16, 81920)
    CS2B = Ar("CS2B", [128, 1024], BF16, 83968)
    ROWF = Ar("ROWF", [1, 4096], F32, 86016)
    ROWF2 = Ar("ROWF2", [1, 1024], F32, 102400)
    AW = Ar("AW", [128, M2 * 1024], BF16, 0)
    CSC = Ar("CSC", [128, 4096], BF16, 49152)
    CSD = Ar("CSD", [128, 4096], BF16, 57344)

    pbk = [nc.alloc_psum_tensor(f"pbk{i}", [128, 512], F32) for i in range(5)]
    ptr2 = [nc.alloc_psum_tensor(f"ptr{i}", [128, 1024], BF16) for i in range(2)]
    paux = nc.alloc_psum_tensor("paux", [128, 512], F32)

    Rpb = [Res() for _ in range(5)]
    Rptr2 = [Res(), Res()]
    Raux = Res()
    RB0, RB1 = Res(), Res()
    RHT, RQT, RKT, RVT, RRS = Res(), Res(), Res(), Res(), Res()
    RPM = [Res() for _ in range(16)]
    RPT = [Res(), Res(), Res(), Res()]
    ROT = [Res(), Res()]
    RDIAG = [Res(), Res()]
    RKTM = [Res(), Res()]
    RTMS = [Res(), Res()]
    RSTATS = Res()
    RGT, RWF2, RFF = Res(), Res(), Res()
    RPAN1 = [Res(), Res()]
    RHT2, RWD = Res(), Res()
    RPAN2 = [Res(), Res(), Res(), Res()]
    RSIL = [Res(), Res()]
    ROS = [Res(), Res()]
    RCS, RCS2, RSTG, RAW = Res(), Res(), Res(), Res()
    RCSB, RCS2B = Res(), Res()
    RCSC, RCSD = Res(), Res()
    RROWF, RROWF2 = Res(), Res()
    RWQ, RWO, RWK, RWV, RCT, RST = Res(), Res(), Res(), Res(), Res(), Res()
    RKTH, RVTH = Res(), Res()
    RDMP = Res()

    tcst = []
    for dst, src in [(IDB, C["ident_bf"]), (IDF, C["ident_f32"]),
                     (ONE1, C["ones1"]), (BAND3, C["band3"]),
                     (BAND2, C["band2"]), (BAND1, C["band1"])]:
        tcst.append(P.dma("P", "scv", lambda e, d=dst, s=src: e.dma_start(d[:], s),
                          tag="cst"))

    def colload(dst, src, n, slot):
        tr_ = P.dma("P", "scv", lambda e, s=src: e.dma_start(
            ROWF[0:1, 0:1024], s), deps=RROWF.wdep(), tag="colrow")
        RROWF.write(tr_)
        trl = None
        for j in range(n):
            dep = [tr_, tcst[1]]
            if j == 0:
                dep += Rpb[2].wdep()
            trl = P.op("T", lambda e, jj=j: e.transpose(
                pbk[2][:, jj:jj + 1], ROWF[0:1, jj * 128:(jj + 1) * 128],
                IDF[0:1, 0:1]), deps=dep, tag="coltr")
        Rpb[2].write(trl)
        tc = P.op("V", lambda e, d=dst: e.tensor_copy(d[:, 0:n],
                                                      pbk[2][:, 0:n]),
                  deps=[trl], tag="colev")
        Rpb[2].read(tc)
        RROWF.read(tc)
        return tc

    tg1 = colload(G1C, W["g1"], 8, 0)
    tb1 = colload(B1C, W["b1"], 8, 1)
    tg2 = colload(G2C, W["g2"], 8, 2)
    tb2 = colload(B2C, W["b2"], 8, 3)

    COLS4 = [tg1, tb1, tg2, tb2]
    tb1b = P.op("V", lambda e: e.tensor_copy(B1CB[:], B1C[:]), deps=COLS4,
                tag="b1cb")
    tb2b = P.op("V", lambda e: e.tensor_copy(B2CB[:], B2C[:]), deps=COLS4,
                tag="b2cb")

    def conv_attn_w(wsrc, wdst, n_out, gtok, gcol, extra_scale, bsrc, brow_off,
                    bscale, row_off_f):
        nstr = max(1, n_out // 512)
        strw = min(512, n_out)
        tbr = P.dma("P", "srow", lambda e: e.dma_start(
            ROWF[0:1, row_off_f:row_off_f + n_out], bsrc), deps=RROWF.wdep(),
            tag="brow")
        RROWF.write(tbr)
        out_toks = []
        mm_last = [None] * nstr
        for kc in range(KC):
            buf = CSA if kc % 2 == 0 else CSB
            rbuf = RCS if kc % 2 == 0 else RCSB
            sem = "scv2a" if kc % 2 == 0 else "scv2b"
            tl = P.dma("P", sem, lambda e, k=kc, b=buf: e.dma_start(
                b[:, 0:n_out], wsrc[k * 128:(k + 1) * 128, :]),
                deps=rbuf.wdep(), tag="wload")
            rbuf.write(tl)
            fold_toks = []
            for s in range(nstr):
                dep = [tl, tb1b]
                if kc == 0:
                    dep += Rpb[3 + s].wdep()
                else:
                    dep.append(mm_last[s])
                tm = P.op("T", lambda e, k=kc, ss=s, b=buf: e.matmul(
                    pbk[3 + ss][0:1, 0:strw], B1CB[:, k:k + 1],
                    b[:, ss * strw:ss * strw + strw],
                    start=(k == 0), stop=(k == KC - 1)), deps=dep, tag="bfold")
                rbuf.read(tm)
                mm_last[s] = tm
                fold_toks.append(tm)
            if gcol is not None:
                ts_ = P.op("V", lambda e, k=kc, b=buf: e.tensor_scalar(
                    out=b[:, 0:n_out], in0=b[:, 0:n_out],
                    scalar1=gcol[:, k:k + 1], scalar2=float(extra_scale),
                    op0=ALU.mult, op1=ALU.mult),
                    deps=[tl] + gtok + fold_toks, tag="wscale")
                rbuf.write(ts_)
                tl = ts_
            tw = P.dma("Y", "scw", lambda e, k=kc, b=buf: e.dma_start(
                wdst[k * 128:(k + 1) * 128, :], b[:, 0:n_out]), deps=[tl],
                tag="wstore")
            rbuf.read(tw)
            out_toks.append(tw)
        for s in range(nstr):
            Rpb[3 + s].write(mm_last[s])
            te = P.op("V", lambda e, ss=s: e.scalar_tensor_tensor(
                out=ROWF2[0:1, 0:strw],
                in0=ROWF[0:1, row_off_f + ss * strw:row_off_f + ss * strw + strw],
                scalar=1.0, in1=pbk[3 + ss][0:1, 0:strw],
                op0=ALU.mult, op1=ALU.add),
                deps=[mm_last[s], tbr] + RROWF2.wdep(), tag="bevac")
            tsc = P.op("V", lambda e, ss=s: e.tensor_scalar(
                out=ROWF2[0:1, 0:strw], in0=ROWF2[0:1, 0:strw],
                scalar1=float(bscale), scalar2=None, op0=ALU.mult),
                deps=[te], tag="bscale")
            te = tsc
            Rpb[3 + s].read(te)
            RROWF2.write(te)
            tc = P.op("V", lambda e, ss=s: e.tensor_copy(
                ROWB[0:1, brow_off + ss * strw:brow_off + ss * strw + strw],
                ROWF2[0:1, 0:strw]), deps=[te], tag="bcast")
            tlo = P.op("V", lambda e, ss=s: e.tensor_tensor(
                out=ROWL[0:1, brow_off + ss * strw:brow_off + ss * strw + strw],
                in0=ROWF2[0:1, 0:strw],
                in1=ROWB[0:1, brow_off + ss * strw:brow_off + ss * strw + strw],
                op=ALU.subtract), deps=[tc], tag="blo")
            RROWF2.read(tlo)
            out_toks.append(tlo)
        return out_toks

    cv_wq = conv_attn_w(W["wq"], s_wq, D, COLS4, G1C, 0.125, W["bq"], 0, 0.125, 0)
    cv_wk = conv_attn_w(W["wk"], s_wk, DKV, COLS4, G1C, 1.0, W["bk"], 3072, 1.0, 1024)
    cv_wv = conv_attn_w(W["wv"], s_wv, DKV, COLS4, G1C, 1.0, W["bv"], 3328, 1.0, 1280)

    cv_wo = []
    for kc in range(KC):
        buf = CSA if kc % 2 == 0 else CSB
        rbuf = RCS if kc % 2 == 0 else RCSB
        sem = "scv2a" if kc % 2 == 0 else "scv2b"
        tl = P.dma("P", sem, lambda e, k=kc, b=buf: e.dma_start(
            b[:, 0:D], W["wo"][k * 128:(k + 1) * 128, :]), deps=rbuf.wdep(),
            tag="woload")
        rbuf.write(tl)
        tw = P.dma("Y", "scw", lambda e, k=kc, b=buf: e.dma_start(
            s_wo[k * 128:(k + 1) * 128, :], b[:, 0:D]), deps=[tl], tag="wostore")
        rbuf.read(tw)
        cv_wo.append(tw)
    tbo = P.dma("P", "srow", lambda e: e.dma_start(ROWF[0:1, 1536:2560], W["bo"]),
                deps=RROWF.wdep(), tag="borow")
    RROWF.write(tbo)
    tboc = P.op("V", lambda e: e.tensor_copy(ROWB[0:1, 1024:2048],
                                             ROWF[0:1, 1536:2560]), deps=[tbo],
                tag="bocast")
    tboc = P.op("V", lambda e: e.tensor_tensor(
        out=ROWL[0:1, 1024:2048], in0=ROWF[0:1, 1536:2560],
        in1=ROWB[0:1, 1024:2048], op=ALU.subtract), deps=[tboc], tag="bolo")
    RROWF.read(tboc)
    tbf2 = P.dma("P", "srow", lambda e: e.dma_start(ROWF[0:1, 2560:3584],
                                                   W["bf2"]),
                 deps=RROWF.wdep(), tag="bf2row")
    RROWF.write(tbf2)
    tbf2c = P.op("V", lambda e: e.tensor_copy(ROWB[0:1, 2048:3072],
                                              ROWF[0:1, 2560:3584]), deps=[tbf2],
                 tag="bf2cast")
    tbf2c = P.op("V", lambda e: e.tensor_tensor(
        out=ROWL[0:1, 2048:3072], in0=ROWF[0:1, 2560:3584],
        in1=ROWB[0:1, 2048:3072], op=ALU.subtract), deps=[tbf2c], tag="bf2lo")
    RROWF.read(tbf2c)
    ROWS_DONE = cv_wq + cv_wk + cv_wv + [tboc, tbf2c] + tcst

    stg_loads = []
    for kc in range(KC):
        stg_loads.append(P.dma("P", f"sstg{kc}", lambda e, k=kc: e.dma_start(
            STG[:, k * 4096:(k + 1) * 4096], W["wf1"][k * 128:(k + 1) * 128, :]),
            deps=(RSTG.wdep() if kc == 0 else []), tag="wf1load"))
    tbf1 = P.dma("P", "srow", lambda e: e.dma_start(ROWF[0:1, 0:4096], W["bf1"]),
                 deps=RROWF.wdep() + [tboc, tbf2c], tag="bf1row")
    RROWF.write(tbf1)
    bf1_toks = []
    bf1_mms = []
    bf1_bykc = [[] for _ in range(KC)]
    for s in range(8):
        mm_last = None
        for kc in range(KC):
            dep = [stg_loads[kc], tb2b]
            if kc == 0:
                dep += Rpb[3].wdep()
            mm_last = P.op("T", lambda e, k=kc, ss=s: e.matmul(
                pbk[3][0:1, 0:512], B2CB[:, k:k + 1],
                STG[:, k * 4096 + ss * 512:k * 4096 + ss * 512 + 512],
                start=(k == 0), stop=(k == KC - 1)), deps=dep, tag="bf1fold")
            bf1_mms.append(mm_last)
            bf1_bykc[kc].append(mm_last)
        Rpb[3].write(mm_last)
        te = P.op("V", lambda e, ss=s: e.scalar_tensor_tensor(
            out=ROWF2[0:1, 0:512], in0=ROWF[0:1, ss * 512:ss * 512 + 512],
            scalar=1.0, in1=pbk[3][0:1, 0:512], op0=ALU.mult, op1=ALU.add),
            deps=[mm_last, tbf1] + RROWF2.wdep(), tag="bf1evac")
        Rpb[3].read(te)
        RROWF2.write(te)
        trl = None
        for j in range(4):
            dep = [te] + tcst
            if j == 0:
                dep += Rpb[4].wdep()
            trl = P.op("T", lambda e, ss=s, jj=j: e.transpose(
                pbk[4][:, jj:jj + 1],
                ROWF2[0:1, jj * 128:(jj + 1) * 128], IDF[0:1, 0:1]), deps=dep,
                tag="bf1tr")
        RROWF2.read(trl)
        Rpb[4].write(trl)
        tc4 = P.op("V", lambda e, ss=s: e.tensor_copy(
            BF1C[:, ss * 4:(ss + 1) * 4], pbk[4][:, 0:4]), deps=[trl],
            tag="bf1col")
        Rpb[4].read(tc4)
        bf1_toks.append(tc4)
    tbf1c = bf1_toks[-1]
    stg_toks = []
    for kc in range(KC):
        ts_ = P.op("V", lambda e, k=kc: e.tensor_scalar(
            out=STG[:, k * 4096:(k + 1) * 4096],
            in0=STG[:, k * 4096:(k + 1) * 4096],
            scalar1=G2C[:, k:k + 1], scalar2=1.0, op0=ALU.mult, op1=ALU.mult),
            deps=[stg_loads[kc]] + COLS4 + bf1_bykc[kc], tag="wf1scale")
        stg_toks.append(ts_)
    RSTG.write(stg_toks[-1])
    stg_v = STG[:].rearrange("p (kc m) -> p kc m", m=4096)
    cv_wf1 = [tbf1c]
    for m in range(M1):
        tw = P.dma("Y", "scw", lambda e, mm=m: e.dma_start(
            s_wf1p[mm].rearrange("p (kc j) -> p kc j", j=128),
            stg_v[:, :, mm * 128:(mm + 1) * 128]), deps=[RSTG.w], tag="wf1exp")
        RSTG.read(tw)
        cv_wf1.append(tw)

    cv_wf2 = []
    prev = [RCS.wdep() + cv_wf1, RCSB.wdep() + cv_wf1]
    for fc in range(M1):
        sl = fc % 2
        buf = CSA if sl == 0 else CSB
        tl = P.dma("P", "scv2a" if sl == 0 else "scv2b",
                   lambda e, f=fc, b=buf: e.dma_start(
            b[:, 0:1024], W["wf2"][f * 128:(f + 1) * 128, :]),
            deps=prev[sl], tag="wf2load")
        tw = P.dma("Y", "scw", lambda e, f=fc, b=buf: e.dma_start(
            s_wf2[f * 128:(f + 1) * 128, :], b[:, 0:1024]),
            deps=[tl], tag="wf2store")
        prev[sl] = [tw]
        cv_wf2.append(tw)
    RCS.write(cv_wf2[-1])
    RCSB.write(cv_wf2[-2])

    CBUFS = [CSA, CSB, CSC, CSD]
    CRES = [RCS, RCSB, RCSC, RCSD]
    CSEMS = ["scv2a", "scv2b", "scv2c", "scv2d"]

    def conv_t_weight(wsrc, wdst, npan):
        toks = []
        for hp in range(npan):
            par = hp % 2
            b4 = hp % 4
            buf = CBUFS[b4]
            rbuf = CRES[b4]
            obuf = CS2 if par == 0 else CS2B
            orb = RCS2 if par == 0 else RCS2B
            tl = P.dma("P", CSEMS[b4],
                       lambda e, h=hp, b=buf: e.dma_start(
                b[:, 0:D], wsrc[h * 128:(h + 1) * 128, :]),
                deps=rbuf.wdep() + (cv_wf1 if hp in (2, 3) else []),
                tag="qload")
            rbuf.write(tl)
            trl = None
            for j in range(KC):
                dep = [tl, tcst[0]]
                if j == 0:
                    dep += Rptr2[par].wdep()
                trl = P.op("T", lambda e, jj=j, b=buf, pp=par: e.transpose(
                    ptr2[pp][:, jj * 128:(jj + 1) * 128],
                    b[:, jj * 128:(jj + 1) * 128], IDB[:]), deps=dep, tag="qtr")
            Rptr2[par].write(trl)
            tev = P.op("V", lambda e, o=obuf, pp=par: e.tensor_copy(
                o[:], ptr2[pp][:]), deps=[trl] + orb.wdep(), tag="qevac")
            Rptr2[par].read(tev)
            orb.write(tev)
            rbuf.read(tev)
            tw = P.dma("Y", "scw", lambda e, h=hp, o=obuf: e.dma_start(
                wdst[h], o[:]), deps=[tev], tag="qstore")
            orb.read(tw)
            toks.append(tw)
        return toks

    cv_wg = conv_t_weight(W["wg"], s_wgt, M2)
    cv_wu = conv_t_weight(W["wu"], s_wut, M2)

    aw_toks = []
    aw_first = RAW.wdep()
    for np_ in range(KC):
        par = np_ % 2
        b4 = np_ % 4
        buf = CBUFS[b4]
        rbuf = CRES[b4]
        tl = P.dma("P", CSEMS[b4],
                   lambda e, n=np_, b=buf: e.dma_start(
            b[:, 0:DF2], W["wd"][n * 128:(n + 1) * 128, :]),
            deps=rbuf.wdep() + (cv_wf1 if np_ in (2, 3) else []),
            tag="wdload")
        rbuf.write(tl)
        for grp in range(3):
            gpar = (np_ * 3 + grp) % 2
            trl = None
            for jj in range(8):
                kf = grp * 8 + jj
                dep = [tl]
                if jj == 0:
                    dep += Rptr2[gpar].wdep()
                trl = P.op("T", lambda e, k=kf, j=jj, b=buf, pp=gpar: e.transpose(
                    ptr2[pp][:, j * 128:(j + 1) * 128],
                    b[:, k * 128:(k + 1) * 128], IDB[:]), deps=dep, tag="wdtr")
            Rptr2[gpar].write(trl)
            aw_v = AW[:].rearrange("p (kf t) -> p kf t", t=1024)
            tev = P.op("V", lambda e, g=grp, n=np_, pp=gpar: e.tensor_copy(
                aw_v[:, g * 8:(g + 1) * 8, n * 128:(n + 1) * 128],
                ptr2[pp][:].rearrange("p (a b) -> p a b", b=128)),
                deps=[trl] + (aw_first if np_ == 0 and grp == 0 else []),
                tag="wdevac")
            Rptr2[gpar].read(tev)
            rbuf.read(tev)
            aw_toks.append(tev)
    RAW.write(aw_toks[-1])
    cv_wd = []
    for kf in range(M2):
        tw = P.dma("Y", "scw", lambda e, k=kf: e.dma_start(
            s_wdt[k * 128:(k + 1) * 128, :], AW[:, k * 1024:(k + 1) * 1024]),
            deps=aw_toks, tag="wdstore")
        RAW.read(tw)
        cv_wd.append(tw)

    CONV_DONE = (cv_wq + cv_wk + cv_wv + cv_wo + cv_wf1 + cv_wf2 + cv_wg +
                 cv_wu + cv_wd + ROWS_DONE + [tbf1c] + tcst + [tg1, tb1, tg2, tb2])

    def ln_stat_tile(src, t, dmp, deps, first):
        sl = src[:, t * 1024:(t + 1) * 1024]
        sdep = RSTATS.wdep() if first else []
        t1 = P.op("V", lambda e, s=sl, tt=t: e.reduce_sum(
            STATS[:, tt:tt + 1], s, axis=AX.X),
            deps=deps + sdep, tag="lnsum")
        t2 = P.op("S", lambda e, s=sl, tt=t, d=dmp: e.activation(
            d[:], s, AF.Square, accum_out=STATS[:, 8 + tt:9 + tt]),
            deps=deps + sdep + RDMP.wdep(), tag="lnsq")
        RDMP.write(t2)
        return [t1, t2]

    def ln_stats_finish(toks, want_nmr=False, lo=0, hi=8):
        n = hi - lo
        tmu = P.op("V", lambda e: e.tensor_scalar(
            out=STATS[:, 16 + lo:16 + hi], in0=STATS[:, lo:hi],
            scalar1=1.0 / 1024.0,
            scalar2=None, op0=ALU.mult), deps=toks, tag="lnmu")
        te2 = P.op("V", lambda e: e.tensor_scalar(
            out=STATS[:, 24 + lo:24 + hi], in0=STATS[:, 8 + lo:8 + hi],
            scalar1=1.0 / 1024.0,
            scalar2=None, op0=ALU.mult), deps=toks, tag="lne2")
        tm2 = P.op("V", lambda e: e.tensor_tensor(
            out=STATS[:, 32 + lo:32 + hi], in0=STATS[:, 16 + lo:16 + hi],
            in1=STATS[:, 16 + lo:16 + hi],
            op=ALU.mult), deps=[tmu], tag="lnm2")
        tvar = P.op("V", lambda e: e.scalar_tensor_tensor(
            out=STATS[:, 24 + lo:24 + hi], in0=STATS[:, 24 + lo:24 + hi],
            scalar=float(EPS),
            in1=STATS[:, 32 + lo:32 + hi], op0=ALU.add, op1=ALU.subtract),
            deps=[te2, tm2], tag="lnvar")
        tsd = P.op("S", lambda e: e.activation(
            STATS[:, 32 + lo:32 + hi], STATS[:, 24 + lo:24 + hi], AF.Sqrt),
            deps=[tvar], tag="lnsd")
        trv0 = P.op("V", lambda e: e.reciprocal(
            STATS[:, 40 + lo:40 + hi], STATS[:, 32 + lo:32 + hi]),
            deps=[tsd], tag="lnrinv")
        tn1 = P.op("V", lambda e: e.tensor_tensor(
            out=STATS[:, 56 + lo:56 + hi], in0=STATS[:, 24 + lo:24 + hi],
            in1=STATS[:, 40 + lo:40 + hi],
            op=ALU.mult), deps=[trv0], tag="lnnw1")
        tn2 = P.op("V", lambda e: e.tensor_tensor(
            out=STATS[:, 56 + lo:56 + hi], in0=STATS[:, 56 + lo:56 + hi],
            in1=STATS[:, 40 + lo:40 + hi],
            op=ALU.mult), deps=[tn1], tag="lnnw2")
        tn3 = P.op("V", lambda e: e.tensor_scalar(
            out=STATS[:, 56 + lo:56 + hi], in0=STATS[:, 56 + lo:56 + hi],
            scalar1=-0.5,
            scalar2=1.5, op0=ALU.mult, op1=ALU.add), deps=[tn2], tag="lnnw3")
        trv = P.op("V", lambda e: e.tensor_tensor(
            out=STATS[:, 40 + lo:40 + hi], in0=STATS[:, 40 + lo:40 + hi],
            in1=STATS[:, 56 + lo:56 + hi],
            op=ALU.mult), deps=[tn3], tag="lnnw4")
        RSTATS.write(trv)
        if want_nmr:
            tnm = P.op("V", lambda e: e.scalar_tensor_tensor(
                out=STATS[:, 48 + lo:48 + hi], in0=STATS[:, 16 + lo:16 + hi],
                scalar=-1.0,
                in1=STATS[:, 40 + lo:40 + hi], op0=ALU.mult, op1=ALU.mult),
                deps=[trv],
                tag="lnnmr")
            RSTATS.read(tnm)
            return trv, tnm
        return trv, None

    def flip_tile(srcbuf, srccols, tt, dst, deps, tag):
        pb_ = tt % 2
        trl = None
        for j in range(KC):
            dep = list(deps)
            if j == 0:
                dep += Rptr2[pb_].wdep()
            trl = P.op("T", lambda e, jj=j, b=pb_: e.transpose(
                ptr2[b][:, jj * 128:(jj + 1) * 128],
                srcbuf[:, srccols + jj * 128:srccols + (jj + 1) * 128], IDB[:]),
                deps=dep, tag=tag + "tr")
        Rptr2[pb_].write(trl)
        dv = dst[:].rearrange("p (j t) -> p j t", t=1024)
        tev = P.op("V", lambda e, tt_=tt, b=pb_: e.tensor_copy(
            dv[:, :, tt_ * 128:(tt_ + 1) * 128],
            ptr2[b][:].rearrange("p (a b) -> p a b", b=128)), deps=[trl],
            tag=tag + "ev")
        Rptr2[pb_].read(tev)
        return tev

    last_out_dma = None
    prev_chunk_done = list(CONV_DONE)
    prev_b0_free = []
    TAPREG = {}

    def _reg_taps(**kw):
        TAPREG.update(kw)

    _reg_taps(B0=(B0, F32), B1=(B1, F32), HT=(HT, BF16), QT=(QT, BF16),
              KT=(KT, BF16), VT=(VT, BF16), GT=(GT, BF16), HT2=(HT2, BF16),
              STATS=(STATS, F32), SMX=(SMX, F32), OTB=(OTB, BF16),
              PMASK=(PMASK, BF16), TMS1=(TMS1, BF16), FFACC=(FFACC, F32),
              PT=(PT, BF16), DIAG=(DIAG, BF16),
              ROWB=(ROWB, BF16), BF1C=(BF1C, F32), WF2SB=(WF2SB, BF16),
              WDSB=(WDSB, BF16))

    def chunk_loop():
        nonlocal last_out_dma, prev_chunk_done
        phase_end("conv", -1, CONV_DONE)
        for ch in range(dev_nch):
            chunk_body(ch)

    def chunk_body(ch):
        nonlocal last_out_dma, prev_chunk_done
        ch_half = ch % 2
        new_seq = (ch_half == 0)
        CHB = [prev_chunk_done]

        x_toks = []
        for t in range(NT):
            dep = prev_b0_free[t] if prev_b0_free else CHB
            x_toks.append(P.dma("P", f"sx{t}", lambda e, tt=t: e.dma_start(
                B0[:, tt * 1024:(tt + 1) * 1024],
                xs[ch * 1024 + tt * 128:ch * 1024 + (tt + 1) * 128, :]),
                deps=dep, tag="xload"))
        wq_toks, wk_toks, wv_toks = [], [], []
        for kc in range(KC):
            wq_toks.append(P.dma("Y", "swq", lambda e, k=kc: e.dma_start(
                WQSB[:, k * 1024:(k + 1) * 1024], s_wq[k * 128:(k + 1) * 128, :]),
                deps=CHB, tag="wqld"))
            wk_toks.append(P.dma("Y", "swq", lambda e, k=kc: e.dma_start(
                WKSB[:, k * 256:(k + 1) * 256], s_wk[k * 128:(k + 1) * 128, :]),
                deps=CHB, tag="wkld"))
            wv_toks.append(P.dma("Y", "swq", lambda e, k=kc: e.dma_start(
                WVSB[:, k * 256:(k + 1) * 256], s_wv[k * 128:(k + 1) * 128, :]),
                deps=CHB, tag="wvld"))

        def table_load(qtr, extra):
            pos0 = ch_half * 1024 + qtr * 256
            ctv = C["costab"][pos0:pos0 + 256, :].rearrange(
                "(b p) d -> p b d", p=128)
            stv = C["sintab"][pos0:pos0 + 256, :].rearrange(
                "(b p) d -> p b d", p=128)
            tc = P.dma("P", "sct", lambda e, v=ctv: e.dma_start(
                CTB[:].rearrange("p (b d) -> p b d", d=1024), v),
                deps=CHB + extra, tag="ctld")
            ts_ = P.dma("P", "sct", lambda e, v=stv: e.dma_start(
                STB[:].rearrange("p (b d) -> p b d", d=1024), v),
                deps=CHB + extra, tag="stld")
            return [tc, ts_]

        tab_toks = table_load(0, [])

        halo_r = []
        if not new_seq:
            ktv = KT[:].rearrange("p (j a) -> p j a", a=1280)
            t1_ = P.op("V", lambda e, v=ktv: e.tensor_copy(
                v[:, :, 0:256], KTH[:].rearrange("p (j a) -> p j a", a=256)),
                deps=CHB, tag="halork")
            t2_ = P.op("V", lambda e: e.tensor_copy(VT[:, 0:512], VTH[:]),
                       deps=CHB, tag="halorv")
            halo_r = [t1_, t2_]

        ht_toks = []
        for half in range(2):
            lntoks = []
            for t in range(half * 4, half * 4 + 4):
                lntoks += ln_stat_tile(B0, t, DMP1, [x_toks[t]],
                                       half == 0 and t == 0)
            trv, _ = ln_stats_finish(lntoks, lo=half * 4, hi=half * 4 + 4)
            for t in range(half * 4, half * 4 + 4):
                sl = t % 2
                tap_ = P.op("V", lambda e, tt=t, s=sl: e.tensor_scalar(
                    out=TMS1[:, s * 1024:(s + 1) * 1024],
                    in0=B0[:, tt * 1024:(tt + 1) * 1024],
                    scalar1=STATS[:, 16 + tt:17 + tt],
                    scalar2=STATS[:, 40 + tt:41 + tt],
                    op0=ALU.subtract, op1=ALU.mult),
                    deps=[trv, x_toks[t]] + RTMS[sl].wdep(), tag="ln1ap")
                RSTATS.read(tap_)
                RTMS[sl].write(tap_)
                tfl = flip_tile(TMS1, sl * 1024, t, HT,
                                [tap_] + (RHT.wdep() if t == 0 else []), "h1")
                RTMS[sl].read(tfl)
                ht_toks.append(tfl)
        RHT.write(ht_toks[-1])
        phase_end("a0", ch, ht_toks)

        kt_toks, vt_toks, qt_toks = [], [], []
        last_rope = []
        for t in range(NT):
            if t in (2, 4, 6):
                tab_toks = table_load(t // 2, last_rope)
            tb = t % 2
            qb = [pbk[0], pbk[1]] if t % 2 == 0 else [pbk[2], pbk[3]]
            qres = [Rpb[0], Rpb[1]] if t % 2 == 0 else [Rpb[2], Rpb[3]]
            qmm = [None, None]
            for s in range(2):
                rv = ROWB[0:1, s * 512:(s + 1) * 512].rearrange(
                    "a (e p j) -> a p e j", e=2, p=4, j=64)
                rvl = ROWL[0:1, s * 512:(s + 1) * 512].rearrange(
                    "a (e p j) -> a p e j", e=2, p=4, j=64)
                qmm[s] = P.op("T", lambda e, ss=s, q=qb, r=rv: e.matmul(
                    q[ss][:], ONE1[:], r, start=True, stop=False),
                    deps=ROWS_DONE + qres[s].wdep(), tag="qseed")
                qmm[s] = P.op("T", lambda e, ss=s, q=qb, r=rvl: e.matmul(
                    q[ss][:], ONE1[:], r, start=False, stop=False),
                    deps=[], tag="qseedlo")
            vmm = P.op("T", lambda e: e.matmul(
                pbk[4][:, 0:256], ONE1[:], ROWB[0:1, 3328:3584],
                start=True, stop=False), deps=ROWS_DONE + Rpb[4].wdep(),
                tag="vseed")
            vmm = P.op("T", lambda e: e.matmul(
                pbk[4][:, 0:256], ONE1[:], ROWL[0:1, 3328:3584],
                start=False, stop=False), deps=[], tag="vseedlo")
            for kc in range(KC):
                for s in range(2):
                    rv = WQSB[:, kc * 1024 + s * 512:kc * 1024 + (s + 1) * 512]
                    rv = rv.rearrange("a (e p j) -> a p e j", e=2, p=4, j=64)
                    qmm[s] = P.op("T", lambda e, k=kc, ss=s, q=qb, tt=t, r=rv:
                                  e.matmul(
                        q[ss][:],
                        HT[:, k * 1024 + tt * 128:k * 1024 + (tt + 1) * 128],
                        r, start=False, stop=(k == KC - 1)),
                        deps=[ht_toks[t], wv_toks[-1]], tag="qmm")
                vmm = P.op("T", lambda e, k=kc, tt=t: e.matmul(
                    pbk[4][:, 0:256],
                    HT[:, k * 1024 + tt * 128:k * 1024 + (tt + 1) * 128],
                    WVSB[:, k * 256:(k + 1) * 256], start=False,
                    stop=(k == KC - 1)),
                    deps=[ht_toks[t], wv_toks[-1]], tag="vmm")
            for s in range(2):
                qres[s].write(qmm[s])
            Rpb[4].write(vmm)
            tvv = P.op("V", lambda e, tt=t: e.tensor_copy(
                VT[:, (tt + 2) * 256:(tt + 3) * 256], pbk[4][:, 0:256]),
                deps=[vmm] + halo_r + (RVT.wdep() if t == 0 else []), tag="vev")
            Rpb[4].read(tvv)
            vt_toks.append(tvv)
            kmm = P.op("T", lambda e: e.matmul(
                pbk[4][:, 0:256], ONE1[:], ROWB[0:1, 3072:3328],
                start=True, stop=False), deps=ROWS_DONE + [tvv], tag="kseed")
            kmm = P.op("T", lambda e: e.matmul(
                pbk[4][:, 0:256], ONE1[:], ROWL[0:1, 3072:3328],
                start=False, stop=False), deps=[], tag="kseedlo")
            for kc in range(KC):
                kmm = P.op("T", lambda e, k=kc, tt=t: e.matmul(
                    pbk[4][:, 0:256],
                    HT[:, k * 1024 + tt * 128:k * 1024 + (tt + 1) * 128],
                    WKSB[:, k * 256:(k + 1) * 256], start=False,
                    stop=(k == KC - 1)),
                    deps=[ht_toks[t], wv_toks[-1]], tag="kmm")
            Rpb[4].write(kmm)
            sl = t % 2
            rsdep = RRS.wdep()
            ropedeps = []
            for s in range(2):
                ct = CTB[:, tb * 1024 + s * 512:tb * 1024 + (s + 1) * 512]
                st_ = STB[:, tb * 1024 + s * 512:tb * 1024 + (s + 1) * 512]
                t1_ = P.op("V", lambda e, ss=s, q=qb, c=ct: e.tensor_tensor(
                    out=RS[:, ss * 512:(ss + 1) * 512], in0=q[ss][:], in1=c,
                    op=ALU.mult), deps=[qmm[s]] + tab_toks + rsdep, tag="ropet1")
                qv = qb[s][:].rearrange("p (a hh j) -> p a hh j", hh=2, j=32)
                sv = st_.rearrange("p (a hh j) -> p a hh j", hh=2, j=32)
                uv = RS[:, 1024 + s * 512:1024 + (s + 1) * 512].rearrange(
                    "p (a hh j) -> p a hh j", hh=2, j=32)
                tua = P.op("V", lambda e, q=qv, s_=sv, u=uv: e.tensor_tensor(
                    out=u[:, :, 0, :], in0=q[:, :, 1, :], in1=s_[:, :, 0, :],
                    op=ALU.mult), deps=[qmm[s]] + tab_toks + rsdep, tag="ropeua")
                tub = P.op("V", lambda e, q=qv, s_=sv, u=uv: e.tensor_tensor(
                    out=u[:, :, 1, :], in0=q[:, :, 0, :], in1=s_[:, :, 1, :],
                    op=ALU.mult), deps=[qmm[s]] + tab_toks + rsdep, tag="ropeub")
                tadd = P.op("V", lambda e, ss=s, slot=sl: e.tensor_tensor(
                    out=TMS1[:, slot * 1024 + ss * 512:slot * 1024 + (ss + 1) * 512],
                    in0=RS[:, ss * 512:(ss + 1) * 512],
                    in1=RS[:, 1024 + ss * 512:1024 + (ss + 1) * 512],
                    op=ALU.add),
                    deps=[t1_, tua, tub] + (RTMS[sl].wdep() if s == 0 else []),
                    tag="ropeadd")
                qres[s].read(tadd)
                ropedeps.append(tadd)
            RRS.write(ropedeps[-1])
            RTMS[sl].write(ropedeps[-1])
            tfl = flip_tile(TMS1, sl * 1024, t, QT,
                            ropedeps + (RQT.wdep() if t == 0 else []), "qf")
            RTMS[sl].read(tfl)
            qt_toks.append(tfl)
            ct = CTB[:, tb * 1024:tb * 1024 + 256]
            st_ = STB[:, tb * 1024:tb * 1024 + 256]
            rsdep2 = [ropedeps[-1]]
            t1_ = P.op("V", lambda e, c=ct: e.tensor_tensor(
                out=RS[:, 0:256], in0=pbk[4][:, 0:256], in1=c, op=ALU.mult),
                deps=[kmm] + tab_toks + rsdep2, tag="kropet1")
            kv_ = pbk[4][:, 0:256].rearrange("p (a hh j) -> p a hh j", hh=2, j=32)
            sv = st_.rearrange("p (a hh j) -> p a hh j", hh=2, j=32)
            uv = RS[:, 256:512].rearrange("p (a hh j) -> p a hh j", hh=2, j=32)
            tua = P.op("V", lambda e, q=kv_, s_=sv, u=uv: e.tensor_tensor(
                out=u[:, :, 0, :], in0=q[:, :, 1, :], in1=s_[:, :, 0, :],
                op=ALU.mult), deps=[kmm] + tab_toks + rsdep2, tag="kropeua")
            tub = P.op("V", lambda e, q=kv_, s_=sv, u=uv: e.tensor_tensor(
                out=u[:, :, 1, :], in0=q[:, :, 0, :], in1=s_[:, :, 1, :],
                op=ALU.mult), deps=[kmm] + tab_toks + rsdep2, tag="kropeub")
            ksl = t % 2
            tadd = P.op("V", lambda e, s=ksl: e.tensor_tensor(
                out=KTM[:, s * 256:(s + 1) * 256], in0=RS[:, 0:256],
                in1=RS[:, 256:512], op=ALU.add),
                deps=[t1_, tua, tub] + RKTM[ksl].wdep(), tag="kropeadd")
            RRS.write(tadd)
            RKTM[ksl].write(tadd)
            Rpb[4].read(tadd)
            last_rope = [tadd]
            trl = None
            for j in range(2):
                dep = [tadd]
                if j == 0:
                    dep += Rptr2[1].wdep()
                trl = P.op("T", lambda e, jj=j, s=ksl: e.transpose(
                    ptr2[1][:, jj * 128:(jj + 1) * 128],
                    KTM[:, s * 256 + jj * 128:s * 256 + (jj + 1) * 128], IDB[:]),
                    deps=dep, tag="kftr")
            Rptr2[1].write(trl)
            ktv2 = KT[:].rearrange("p (j a) -> p j a", a=1280)
            tev = P.op("V", lambda e, tt=t, v=ktv2: e.tensor_copy(
                v[:, :, (tt + 2) * 128:(tt + 3) * 128],
                ptr2[1][:, 0:256].rearrange("p (a b) -> p a b", b=128)),
                deps=[trl] + halo_r + (RKT.wdep() if t == 0 else []), tag="kfev")
            Rptr2[1].read(tev)
            RKTM[ksl].read(tev)
            kt_toks.append(tev)
        RQT.write(qt_toks[-1])
        RKT.write(kt_toks[-1])
        RVT.write(vt_toks[-1])
        a1_tail = [qt_toks[-1], kt_toks[-1], vt_toks[-1]]
        phase_end("a1", ch, qt_toks + kt_toks + vt_toks)

        a3_toks, u_toks = [], []
        ln2_stat_toks = []
        last_tex = [None] * H
        last_recip = [None, None]
        last_norm = [[], []]
        wo_toks = []
        def a2_stage12(qt, h0, h1, st=None):
            nonlocal last_tex
            if new_seq:
                nk = min(qt + 1, 3)
            else:
                nk = 3
            tidx0 = qt + (3 - nk)
            band = {3: BAND3, 2: BAND2, 1: BAND1}[nk]
            w = nk * 128
            if st is None:
                st = dict(qt=qt, nk=nk, tidx0=tidx0, w=w,
                          stt_toks=[None] * H, av_last=[None] * H,
                          tex_l=[None] * H)
            stt_toks = st["stt_toks"]
            av_last = st["av_last"]
            tex_l = st["tex_l"]
            par = qt % 2
            rs_base = 16 + 16 * par
            rc_base = 48 + 16 * par
            for h in range(h0, h1):
                bank = h % 3
                kvh = h // 4
                qb_ = (h % 4) * 2 + (h // 4) % 2 + (h // 8) * 8
                lhs = QT[(qb_ % 2) * 64:(qb_ % 2) * 64 + 64,
                         (qb_ // 2) * 1024 + qt * 128:(qb_ // 2) * 1024 + (qt + 1) * 128]
                rhs = KT[(kvh % 2) * 64:(kvh % 2) * 64 + 64,
                         (kvh // 2) * 1280 + tidx0 * 128:
                         (kvh // 2) * 1280 + tidx0 * 128 + w]
                tsm = P.op("T", lambda e, bk=bank, l=lhs, r=rhs, ww=w: e.matmul(
                    pbk[bk][:, 0:ww], l, r, start=True, stop=True),
                    deps=[qt_toks[qt], kt_toks[qt]] + halo_r + Rpb[bank].wdep(),
                    tag="smm")
                Rpb[bank].write(tsm)
                tmx = P.op("V", lambda e, bk=bank, hh=h, ww=w: e.reduce_max(
                    SMX[:, hh:hh + 1], pbk[bk][:, 0:ww], axis=AX.X, negate=True),
                    deps=[tsm, last_tex[h]], tag="smax")
                Rpb[bank].read(tmx)
                tex = P.op("S", lambda e, bk=bank, hh=h, ww=w: e.activation(
                    PMASK[:, hh * 384:hh * 384 + ww], pbk[bk][:, 0:ww], AF.Exp,
                    bias=SMX[:, hh:hh + 1], scale=1.0),
                    deps=[tmx] + RPM[h].wdep(), tag="sexp")
                Rpb[bank].read(tex)
                RPM[h].write(tex)
                tex_l[h] = tex
                last_tex[h] = tex
            for h in range(h0, h1):
                tst = P.op("V", lambda e, hh=h, ww=w, bd=band, rb=rs_base:
                           e.scalar_tensor_tensor(
                    out=PMASK[:, hh * 384:hh * 384 + ww],
                    in0=PMASK[:, hh * 384:hh * 384 + ww], scalar=1.0,
                    in1=bd[:, 0:ww], op0=ALU.mult, op1=ALU.mult,
                    accum_out=SMX[:, rb + hh:rb + hh + 1]),
                    deps=[tex_l[h], last_recip[par]], tag="sstt")
                RPM[h].write(tst)
                stt_toks[h] = tst
            return st

        def a2_stage3(st, g):
            nonlocal a3_toks, u_toks, ln2_stat_toks, wo_toks, last_recip, \
                last_norm
            qt = st["qt"]; nk = st["nk"]; tidx0 = st["tidx0"]; w = st["w"]
            stt_toks = st["stt_toks"]; av_last = st["av_last"]
            par = qt % 2
            rs_base = 16 + 16 * par
            rc_base = 48 + 16 * par
            osl = qt % 2
            ot_toks = st.setdefault("ot_toks", [])
            if True:
                trec = P.op("V", lambda e, gg=g, rb=rs_base, rc=rc_base:
                            e.reciprocal(
                    SMX[:, rc + gg * 8:rc + gg * 8 + 8],
                    SMX[:, rb + gg * 8:rb + gg * 8 + 8]),
                    deps=stt_toks[g * 8:(g + 1) * 8] + last_norm[par],
                    tag="srecip")
                if g == 1:
                    last_recip[par] = trec
                    last_norm[par] = []
                av_grp = []
                for h in range(g * 8, (g + 1) * 8):
                    kvh = h // 4
                    tnm = P.op("V", lambda e, hh=h, ww=w, rc=rc_base:
                               e.tensor_scalar_mul(
                        PMASK[:, hh * 384:hh * 384 + ww],
                        PMASK[:, hh * 384:hh * 384 + ww],
                        SMX[:, rc + hh:rc + hh + 1]),
                        deps=[trec, stt_toks[h]], tag="snorm")
                    RPM[h].write(tnm)
                    last_norm[par].append(tnm)
                    ppar = h % 2
                    psl = h % 4
                    trl = None
                    for i in range(nk):
                        dep = [tnm]
                        if i == 0:
                            dep += Rptr2[ppar].wdep()
                        trl = P.op("T", lambda e, hh=h, ii=i, pp=ppar:
                                   e.transpose(
                            ptr2[pp][:, ii * 128:(ii + 1) * 128],
                            PMASK[:, hh * 384 + ii * 128:hh * 384 + (ii + 1) * 128],
                            IDB[:]), deps=dep, tag="ptr")
                    Rptr2[ppar].write(trl)
                    RPM[h].read(trl)
                    tevp = P.op("S", lambda e, pp=ppar, ps=psl, ww=w:
                                e.activation(
                        PT[:, ps * 384:ps * 384 + ww],
                        ptr2[pp][:, 0:ww], AF.Copy),
                        deps=[trl] + RPT[psl].wdep(), tag="ptev")
                    Rptr2[ppar].read(tevp)
                    RPT[psl].write(tevp)
                    avl = None
                    for i in range(nk):
                        dep = [tevp, vt_toks[qt]] + halo_r
                        if i == 0 and h % 2 == 0:
                            dep += Raux.wdep()
                        avl = P.op("T", lambda e, ii=i, hh=h, k=kvh,
                                   t0=tidx0, ps=psl: e.matmul(
                            paux[(hh % 2) * 64:(hh % 2) * 64 + 64,
                                 ((hh // 2) % 4) * 128:((hh // 2) % 4 + 1) * 128],
                            VT[:, (t0 + ii) * 256 + k * 64:(t0 + ii) * 256 + k * 64 + 64],
                            PT[:, ps * 384 + ii * 128:ps * 384 + (ii + 1) * 128],
                            start=(ii == 0), stop=(ii == nk - 1)), deps=dep,
                            tag="avmm")
                    RPT[psl].read(avl)
                    av_last[h] = avl
                    av_grp.append(avl)
                    if h % 2 == 0:
                        Raux.write(avl)
                tev = P.op("S", lambda e, gg=g, o=osl: e.activation(
                    OTB[:, o * 1024 + gg * 512:o * 1024 + (gg + 1) * 512],
                    paux[:, 0:512], AF.Copy),
                    deps=av_grp + (ROT[osl].wdep() if g == 0 else []), tag="avev")
                Raux.read(tev)
                ot_toks.append(tev)
            if g == 0:
                return
            if qt == 0:
                for kc in range(KC):
                    wo_toks.append(P.dma("Y", "swo", lambda e, k=kc: e.dma_start(
                        WOSB[:, k * 1024:(k + 1) * 1024],
                        s_wo[k * 128:(k + 1) * 128, :]),
                        deps=[qt_toks[-1], qmm[0], qmm[1]] + cv_wo, tag="wold"))
            ev_toks = []
            for s in range(2):
                wb = 3 + s
                wos = P.op("T", lambda e, ss=s, b=wb: e.matmul(
                    pbk[b][:], ONE1[:],
                    ROWB[0:1, 1024 + ss * 512:1024 + (ss + 1) * 512],
                    start=True, stop=False),
                    deps=ROWS_DONE + Rpb[wb].wdep(), tag="woseed")
                wol = P.op("T", lambda e, ss=s, b=wb: e.matmul(
                    pbk[b][:], ONE1[:],
                    ROWL[0:1, 1024 + ss * 512:1024 + (ss + 1) * 512],
                    start=False, stop=False), deps=[], tag="woseedlo")
                for p in range(KC):
                    wol = P.op("T", lambda e, pp=p, ss=s, o=osl, b=wb: e.matmul(
                        pbk[b][:],
                        OTB[:, o * 1024 + pp * 128:o * 1024 + (pp + 1) * 128],
                        WOSB[:, pp * 1024 + ss * 512:pp * 1024 + (ss + 1) * 512],
                        start=False, stop=(pp == KC - 1)),
                        deps=[ot_toks[-1], wo_toks[-1]], tag="womm")
                Rpb[wb].write(wol)
                te = P.op("V", lambda e, ss=s, q=qt, b=wb: e.tensor_tensor(
                    out=B1[:, q * 1024 + ss * 512:q * 1024 + (ss + 1) * 512],
                    in0=pbk[b][:],
                    in1=B0[:, q * 1024 + ss * 512:q * 1024 + (ss + 1) * 512],
                    op=ALU.add), deps=[wol], tag="r1ev")
                Rpb[wb].read(te)
                ev_toks.append(te)
            ROT[osl].read(ev_toks[-1])
            tu = P.op("V", lambda e, q=qt: e.tensor_tensor(
                out=B0[:, q * 1024:(q + 1) * 1024],
                in0=B0[:, q * 1024:(q + 1) * 1024],
                in1=B1[:, q * 1024:(q + 1) * 1024], op=ALU.add),
                deps=ev_toks, tag="uev")
            a3_toks += ev_toks
            u_toks.append(tu)
            ln2_stat_toks += ln_stat_tile(B1, qt, DMP1, ev_toks, qt == 0)

        phase_end("a3", ch, u_toks + a3_toks)


        st_prev = a2_stage12(0, 0, 8)
        a2_stage12(0, 8, 16, st_prev)
        for qt in range(1, NT):
            a2_stage3(st_prev, 0)
            st_cur = a2_stage12(qt, 0, 8)
            a2_stage3(st_prev, 1)
            a2_stage12(qt, 8, 16, st_cur)
            st_prev = st_cur
        a2_stage3(st_prev, 0)
        a2_stage3(st_prev, 1)

        halo_s = []
        if ch_half == 0:
            ktv2 = KT[:].rearrange("p (j a) -> p j a", a=1280)
            t1_ = P.op("V", lambda e, v=ktv2: e.tensor_copy(
                KTH[:].rearrange("p (j a) -> p j a", a=256), v[:, :, 1024:1280]),
                deps=[kt_toks[-1], kt_toks[-2]], tag="halosk")
            t2_ = P.op("V", lambda e: e.tensor_copy(VTH[:], VT[:, 2048:2560]),
                       deps=[vt_toks[-1], vt_toks[-2]], tag="halosv")
            halo_s = [t1_, t2_]
            RKTH.write(t1_)
            RVTH.write(t2_)

        ht2_toks = []
        for half in range(2):
            trv2, _ = ln_stats_finish(ln2_stat_toks[half * 8:half * 8 + 8],
                                      lo=half * 4, hi=half * 4 + 4)
            for t in range(half * 4, half * 4 + 4):
                sl = t % 2
                tap_ = P.op("V", lambda e, tt=t, s=sl: e.tensor_scalar(
                    out=TMS1[:, s * 1024:(s + 1) * 1024],
                    in0=B1[:, tt * 1024:(tt + 1) * 1024],
                    scalar1=STATS[:, 16 + tt:17 + tt],
                    scalar2=STATS[:, 40 + tt:41 + tt],
                    op0=ALU.subtract, op1=ALU.mult),
                    deps=[trv2] + RTMS[sl].wdep(), tag="ln2ap")
                RSTATS.read(tap_)
                RTMS[sl].write(tap_)
                tfl = flip_tile(TMS1, sl * 1024, t, HT,
                                [tap_] + (RHT.wdep() if t == 0 else []), "h2")
                RTMS[sl].read(tfl)
                ht2_toks.append(tfl)
        RHT.write(ht2_toks[-1])

        phase_end("ln2", ch, ht2_toks)
        g1_done = (ht2_toks + halo_s + u_toks +
                   [qt_toks[-1], kt_toks[-1], vt_toks[-1]])

        ff_evac_toks = []
        ln3_stat_toks = []
        half_tail = []
        for hf in range(2):
            wf2_toks = []
            for mi in range(16):
                m = hf * 16 + mi
                wf2_toks.append(P.dma("Y", "swf2", lambda e, mm_=m, mi_=mi:
                                      e.dma_start(
                    WF2SB[:, mi_ * 1024:(mi_ + 1) * 1024],
                    s_wf2[mm_ * 128:(mm_ + 1) * 128, :]),
                    deps=(a1_tail if hf == 0 else half_tail) + cv_wf2,
                    tag="wf2ld"))
            gt_toks = []
            for mi in range(16):
                m = hf * 16 + mi
                psl = mi % 2
                tpl = P.dma("P", "swf1a" if psl == 0 else "swf1b",
                            lambda e, mm_=m, p=psl: e.dma_start(
                                PAN1[:, p * 1024:(p + 1) * 1024], s_wf1p[mm_]),
                            deps=RPAN1[psl].wdep() + cv_wf1 +
                                 ([ht2_toks[-1]] if hf == 0 and mi < 2 else []),
                            tag="wf1pld")
                RPAN1[psl].write(tpl)
                bsel = [Rpb[0], Rpb[1]] if mi % 2 == 0 else [Rpb[2], Rpb[3]]
                bbk = [pbk[0], pbk[1]] if mi % 2 == 0 else [pbk[2], pbk[3]]
                for s in range(2):
                    mm_l = None
                    for kc in range(KC):
                        dep = [tpl, ht2_toks[-1]]
                        if kc == 0:
                            dep += bsel[s].wdep()
                        mm_l = P.op("T", lambda e, k=kc, ss=s, p=psl, b=bbk:
                                    e.matmul(
                            b[ss][:],
                            PAN1[:, p * 1024 + k * 128:p * 1024 + (k + 1) * 128],
                            HT[:, k * 1024 + ss * 512:k * 1024 + (ss + 1) * 512],
                            start=(k == 0), stop=(k == KC - 1)), deps=dep,
                            tag="f1amm")
                    bsel[s].write(mm_l)
                    RPAN1[psl].read(mm_l)
                    tg = P.op("S", lambda e, ss=s, mi_=mi, m_=m, b=bbk:
                              e.activation(
                        GT[:, mi_ * 1024 + ss * 512:mi_ * 1024 + (ss + 1) * 512],
                        b[ss][:], AF.Gelu, bias=BF1C[:, m_:m_ + 1], scale=1.0),
                        deps=[mm_l, tbf1c] +
                             (RGT.wdep() if mi == 0 and s == 0 else []),
                        tag="gelu")
                    bsel[s].read(tg)
                    gt_toks.append(tg)
            RGT.write(gt_toks[-1])
            for t in range(NT):
                bsel = [Rpb[0], Rpb[1]] if t % 2 == 0 else [Rpb[2], Rpb[3]]
                bbk = [pbk[0], pbk[1]] if t % 2 == 0 else [pbk[2], pbk[3]]
                mm_l = [None, None]
                if hf == 0:
                    for s in range(2):
                        mm_l[s] = P.op("T", lambda e, ss=s, b=bbk: e.matmul(
                            b[ss][:], ONE1[:],
                            ROWB[0:1, 2048 + ss * 512:2048 + (ss + 1) * 512],
                            start=True, stop=False),
                            deps=ROWS_DONE + bsel[s].wdep(), tag="f2seed")
                        mm_l[s] = P.op("T", lambda e, ss=s, b=bbk: e.matmul(
                            b[ss][:], ONE1[:],
                            ROWL[0:1, 2048 + ss * 512:2048 + (ss + 1) * 512],
                            start=False, stop=False), deps=[], tag="f2seedlo")
                for mi in range(16):
                    for s in range(2):
                        dep = [gt_toks[-1], wf2_toks[-1]]
                        if mm_l[s] is None:
                            dep += bsel[s].wdep()
                        first = (mm_l[s] is None)
                        mm_l[s] = P.op("T", lambda e, mi_=mi, ss=s, tt=t, b=bbk,
                                       f=first: e.matmul(
                            b[ss][:],
                            GT[:, mi_ * 1024 + tt * 128:mi_ * 1024 + (tt + 1) * 128],
                            WF2SB[:, mi_ * 1024 + ss * 512:mi_ * 1024 + (ss + 1) * 512],
                            start=f, stop=(mi_ == 15)), deps=dep, tag="f1bmm")
                for s in range(2):
                    bsel[s].write(mm_l[s])
                    if hf == 0:
                        te = P.op("V", lambda e, tt=t, ss=s, b=bbk:
                                  e.scalar_tensor_tensor(
                            out=FFACC[:, tt * 1024 + ss * 512:tt * 1024 + (ss + 1) * 512],
                            in0=b[ss][:], scalar=1.0,
                            in1=B0[:, tt * 1024 + ss * 512:tt * 1024 + (ss + 1) * 512],
                            op0=ALU.mult, op1=ALU.add),
                            deps=[mm_l[s], u_toks[t]] +
                                 (RFF.wdep() if t == 0 and s == 0 else []),
                            tag="ffacc")
                    else:
                        te = P.op("V", lambda e, tt=t, ss=s, b=bbk:
                                  e.scalar_tensor_tensor(
                            out=B0[:, tt * 1024 + ss * 512:tt * 1024 + (ss + 1) * 512],
                            in0=b[ss][:], scalar=1.0,
                            in1=FFACC[:, tt * 1024 + ss * 512:tt * 1024 + (ss + 1) * 512],
                            op0=ALU.mult, op1=ALU.add),
                            deps=[mm_l[s], RFF.w, u_toks[t]], tag="preln")
                    bsel[s].read(te)
                    ff_evac_toks.append(te)
                if hf == 1:
                    ln3_stat_toks += ln_stat_tile(
                        B0, t, DMP2, ff_evac_toks[-2:], t == 0)
            if hf == 0:
                RFF.write(ff_evac_toks[-1])
                half_tail = ff_evac_toks[-16:]
        RB0.write(ff_evac_toks[-1])
        pre_toks = ff_evac_toks[16 * 1:][-16:]
        phase_end("ffn1", ch, ff_evac_toks)

        x1_toks, x1t_toks = [], []
        for half in range(2):
            trv3, tnm3 = ln_stats_finish(ln3_stat_toks[half * 8:half * 8 + 8],
                                         want_nmr=True, lo=half * 4,
                                         hi=half * 4 + 4)
            for t in range(half * 4, half * 4 + 4):
                tap_ = P.op("V", lambda e, tt=t: e.tensor_scalar(
                    out=B1[:, tt * 1024:(tt + 1) * 1024],
                    in0=B0[:, tt * 1024:(tt + 1) * 1024],
                    scalar1=STATS[:, 16 + tt:17 + tt],
                    scalar2=STATS[:, 40 + tt:41 + tt],
                    op0=ALU.subtract, op1=ALU.mult), deps=[trv3], tag="ln3v")
                RSTATS.read(tap_)
                x1_toks.append(tap_)
                sl = t % 2
                tap2 = P.op("S", lambda e, tt=t, s=sl: e.activation(
                    TMS2[:, s * 1024:(s + 1) * 1024],
                    B0[:, tt * 1024:(tt + 1) * 1024], AF.Identity,
                    bias=STATS[:, 48 + tt:49 + tt],
                    scale=STATS[:, 40 + tt:41 + tt]),
                    deps=[tnm3, RFF.w] + RTMS[sl].wdep(), tag="ln3s")
                RSTATS.read(tap2)
                RTMS[sl].write(tap2)
                tfl = flip_tile(TMS2, sl * 1024, t, HT,
                                [tap2, gt_toks[-1]] +
                                (RHT.wdep() if t == 0 else []), "x1f")
                RTMS[sl].read(tfl)
                x1t_toks.append(tfl)
        RB1.write(x1_toks[-1])
        RHT.write(x1t_toks[-1])
        phase_end("ln3", ch, x1_toks + x1t_toks)

        wd_toks = []
        h2_toks = []
        for mi in range(M2):
            wd_toks.append(P.dma("Y", "swd", lambda e, k=mi: e.dma_start(
                WDSB[:, k * 1024:(k + 1) * 1024],
                s_wdt[k * 128:(k + 1) * 128, :]),
                deps=(x1t_toks if mi == 0 else []) + cv_wd, tag="wdld"))
            pg, pu = mi % 2, 2 + mi % 2
            tpg = P.dma("Y", "swg" + str(mi % 2), lambda e, m=mi, p=pg:
                        e.dma_start(PAN2[:, p * 1024:(p + 1) * 1024], s_wgt[m]),
                        deps=(x1_toks + x1t_toks if mi < 2 else []) +
                             RPAN2[pg].wdep() + cv_wg, tag="wgld")
            tpu = P.dma("Y", "swu" + str(mi % 2), lambda e, m=mi, p=pu:
                        e.dma_start(PAN2[:, p * 1024:(p + 1) * 1024], s_wut[m]),
                        deps=(x1_toks + x1t_toks if mi < 2 else []) +
                             RPAN2[pu].wdep() + cv_wu, tag="wuld")
            RPAN2[pg].write(tpg)
            RPAN2[pu].write(tpu)
            for s in range(2):
                bg = ((2 * mi + s) % 2) * 2
                bu = bg + 1
                gmm, umm = None, None
                for kc in range(KC):
                    depg = [tpg, x1t_toks[-1]]
                    if kc == 0:
                        depg += Rpb[bg].wdep()
                    gmm = P.op("T", lambda e, k=kc, ss=s, p=pg, b=bg: e.matmul(
                        pbk[b][:],
                        PAN2[:, p * 1024 + k * 128:p * 1024 + (k + 1) * 128],
                        HT[:, k * 1024 + ss * 512:k * 1024 + (ss + 1) * 512],
                        start=(k == 0), stop=(k == KC - 1)), deps=depg,
                        tag="gmm")
                    depu = [tpu, x1t_toks[-1]]
                    if kc == 0:
                        depu += Rpb[bu].wdep()
                    umm = P.op("T", lambda e, k=kc, ss=s, p=pu, b=bu: e.matmul(
                        pbk[b][:],
                        PAN2[:, p * 1024 + k * 128:p * 1024 + (k + 1) * 128],
                        HT[:, k * 1024 + ss * 512:k * 1024 + (ss + 1) * 512],
                        start=(k == 0), stop=(k == KC - 1)), deps=depu,
                        tag="umm")
                Rpb[bg].write(gmm)
                Rpb[bu].write(umm)
                RPAN2[pg].read(gmm)
                RPAN2[pu].read(umm)
                ssl = s
                tsi = P.op("S", lambda e, b=bg, sl_=ssl: e.activation(
                    SIL[:, sl_ * 512:(sl_ + 1) * 512], pbk[b][:], AF.Silu,
                    scale=float(QSC)), deps=[gmm] + RSIL[ssl].wdep(), tag="silu")
                Rpb[bg].read(tsi)
                RSIL[ssl].write(tsi)
                th = P.op("V", lambda e, b=bu, sl_=ssl, m=mi, ss=s:
                          e.tensor_tensor(
                    out=HT2[:, m * 1024 + ss * 512:m * 1024 + (ss + 1) * 512],
                    in0=SIL[:, sl_ * 512:(sl_ + 1) * 512], in1=pbk[b][:],
                    op=ALU.mult),
                    deps=[tsi, umm] +
                         (RHT2.wdep() if mi == 0 and s == 0 else []), tag="hmul")
                Rpb[bu].read(th)
                RSIL[ssl].read(th)
                h2_toks.append(th)
        RHT2.write(h2_toks[-1])
        phase_end("ffn2a", ch, h2_toks)

        dn_toks = []
        ln4_stat_toks = []
        out_toks = []
        b0_free = []
        trv4 = None
        for t in range(NT):
            bsel = [Rpb[0], Rpb[1]] if t % 2 == 0 else [Rpb[2], Rpb[3]]
            bbk = [pbk[0], pbk[1]] if t % 2 == 0 else [pbk[2], pbk[3]]
            mm_l = [None, None]
            for mi in range(M2):
                for s in range(2):
                    dep = [h2_toks[-1], wd_toks[-1]]
                    if mm_l[s] is None:
                        dep += bsel[s].wdep()
                    mm_l[s] = P.op("T", lambda e, m=mi, ss=s, tt=t, b=bbk:
                                   e.matmul(
                        b[ss][:],
                        HT2[:, m * 1024 + tt * 128:m * 1024 + (tt + 1) * 128],
                        WDSB[:, m * 1024 + ss * 512:m * 1024 + (ss + 1) * 512],
                        start=(m == 0), stop=(m == M2 - 1)), deps=dep,
                        tag="dnmm")
            for s in range(2):
                bsel[s].write(mm_l[s])
                te = P.op("V", lambda e, tt=t, ss=s, b=bbk:
                          e.scalar_tensor_tensor(
                    out=B0[:, tt * 1024 + ss * 512:tt * 1024 + (ss + 1) * 512],
                    in0=b[ss][:], scalar=float(QSC * QSC),
                    in1=B1[:, tt * 1024 + ss * 512:tt * 1024 + (ss + 1) * 512],
                    op0=ALU.mult, op1=ALU.add),
                    deps=[mm_l[s], x1_toks[t]], tag="dnev")
                bsel[s].read(te)
                dn_toks.append(te)
            ln4_stat_toks += ln_stat_tile(B0, t, DMP4, dn_toks[-2:], t == 0)
            if t in (3, 7):
                g = t // 4
                trv4, _ = ln_stats_finish(
                    ln4_stat_toks[g * 8:g * 8 + 8], lo=g * 4, hi=g * 4 + 4)
                for tt in range(g * 4, g * 4 + 4):
                    tap_ = P.op("V", lambda e, t_=tt: e.tensor_scalar(
                        out=B1[:, t_ * 1024:(t_ + 1) * 1024],
                        in0=B0[:, t_ * 1024:(t_ + 1) * 1024],
                        scalar1=STATS[:, 16 + t_:17 + t_],
                        scalar2=STATS[:, 40 + t_:41 + t_],
                        op0=ALU.subtract, op1=ALU.mult),
                        deps=[trv4, dn_toks[2 * tt], dn_toks[2 * tt + 1]],
                        tag="ln4ap")
                    RSTATS.read(tap_)
                    td = P.dma("S", "sy", lambda e, t_=tt: e.dma_start(
                        y[ch * 1024 + t_ * 128:ch * 1024 + (t_ + 1) * 128, :],
                        B1[:, t_ * 1024:(t_ + 1) * 1024]), deps=[tap_],
                        tag="yout")
                    out_toks.append(td)
                    b0_free.append([tap_, ln4_stat_toks[2 * tt + 1], td])
                    last_out_dma = td
        RB0.write(dn_toks[-1])
        phase_end("ffn2b", ch, dn_toks)

        prev_chunk_done = dn_toks + out_toks + [trv4]
        prev_b0_free[:] = b0_free

    try:
        chunk_loop()
        stop_toks = [last_out_dma]
    except StopBuild:
        stop_toks = tap_specs[0]
    tap_dmas = []
    for name in DEV.get("taps", ()):
        h, dtp = TAPREG[name]
        cols = int(h.shape[1])
        tout = nc.dram_tensor("tap_" + name, [128, cols], dtp,
                              kind="ExternalOutput").ap()
        tap_dmas.append(P.dma("S", "stap",
                              lambda e, o=tout, hh=h: e.dma_start(o, hh[:]),
                              deps=stop_toks, tag="tap"))
    P.op("Y", lambda e: e.nop(), deps=stop_toks + tap_dmas, tag="final")
    P.build(nc)
    return nc


def kernel(**inputs):
    x = np.ascontiguousarray(inputs["L0_attn_x"], dtype=np.float32)
    consts = host_consts()
    base = {
        "wq": np.ascontiguousarray(inputs["L0_attn_Wq"], dtype=np.float32),
        "wk": np.ascontiguousarray(inputs["L0_attn_Wk"], dtype=np.float32),
        "wv": np.ascontiguousarray(inputs["L0_attn_Wv"], dtype=np.float32),
        "wo": np.ascontiguousarray(inputs["L0_attn_Wo"], dtype=np.float32),
        "wf1": np.ascontiguousarray(inputs["L0_attn_Wf1"], dtype=np.float32),
        "wf2": np.ascontiguousarray(inputs["L0_attn_Wf2"], dtype=np.float32),
        "wg": np.ascontiguousarray(inputs["L0_ffn_w_gate"], dtype=np.int32),
        "wu": np.ascontiguousarray(inputs["L0_ffn_w_up"], dtype=np.int32),
        "wd": np.ascontiguousarray(inputs["L0_ffn_w_down"], dtype=np.int32),
        "bq": np.ascontiguousarray(inputs["L0_attn_bq"].reshape(1, -1),
                                   dtype=np.float32),
        "bk": np.ascontiguousarray(inputs["L0_attn_bk"].reshape(1, -1),
                                   dtype=np.float32),
        "bv": np.ascontiguousarray(inputs["L0_attn_bv"].reshape(1, -1),
                                   dtype=np.float32),
        "bo": np.ascontiguousarray(inputs["L0_attn_bo"].reshape(1, -1),
                                   dtype=np.float32),
        "bf1": np.ascontiguousarray(inputs["L0_attn_bf1"].reshape(1, -1),
                                    dtype=np.float32),
        "bf2": np.ascontiguousarray(inputs["L0_attn_bf2"].reshape(1, -1),
                                    dtype=np.float32),
        "g1": np.ascontiguousarray(inputs["L0_attn_g1"].reshape(1, -1),
                                   dtype=np.float32),
        "b1": np.ascontiguousarray(inputs["L0_attn_b1"].reshape(1, -1),
                                   dtype=np.float32),
        "g2": np.ascontiguousarray(inputs["L0_attn_g2"].reshape(1, -1),
                                   dtype=np.float32),
        "b2": np.ascontiguousarray(inputs["L0_attn_b2"].reshape(1, -1),
                                   dtype=np.float32),
    }
    base.update(consts)
    nc = build_program()
    in_maps = []
    for c in range(NCORE):
        m = dict(base)
        m["xs"] = np.ascontiguousarray(x[2 * c:2 * c + 2].reshape(NCH * CH_T, D))
        in_maps.append(m)
    res = run_bass_kernel_spmd(nc, in_maps, core_ids=list(range(NCORE)))
    out = np.concatenate([r["y"].reshape(2, 2048, D) for r in res.results],
                         axis=0)
    return out


if __name__ == "__main__":
    nc = build_program()
    print("built ok; instructions:", nc.n_instructions())
